# version 02
# baseline (speedup 1.0000x reference)
import numpy as np
import ml_dtypes
from contextlib import ExitStack
import concourse.bass as bass
import concourse.mybir as mybir
from concourse.bass_utils import run_bass_kernel_spmd
from bass_rust import AP as RawAP

F32 = mybir.dt.float32
BF16 = mybir.dt.bfloat16
U32 = mybir.dt.uint32
AF = mybir.ActivationFunctionType
OP = mybir.AluOpType
AX = mybir.AxisListType
BF = ml_dtypes.bfloat16

NCORE = 8
NB = 2
T = 2048
D = 1024
H, DH = 16, 64
TT = T // 128
NCH = D // 128
QB = 512
NQB = T // QB
KT = T // 128
NBUCK, MAXD = 32, 128
ENGS = ("sync", "tensor", "vector", "scalar", "gpsimd")
DMAQS = ("sync",)



class Em:
    def __init__(self, sem):
        self.sem = sem
        self.items = {e: [] for e in ENGS}
        self.val = 0
        self.dirty = {}

    def op(self, eng, fn):
        self.items[eng].append(["op", fn])
        self.dirty[eng] = True

    def dma(self, eng, fn):
        sem = self.sem

        def wrapped(e, fn=fn):
            return fn(e).then_inc(sem, 16)
        self.items[eng].append(["op", wrapped])
        self.val += 16
        return self.val

    def tok(self, eng):
        sem = self.sem
        item = self.items[eng][-1]
        assert item[0] == "op"
        fn = item[1]

        def wrapped(e, fn=fn):
            return fn(e).then_inc(sem, 1)
        item[1] = wrapped
        self.dirty.pop(eng, None)
        self.val += 1
        return self.val

    def wait(self, eng, v):
        self.items[eng].append(["wait", v])

    def barrier(self, cons=ENGS):
        for eng in list(self.dirty):
            self.tok(eng)
        for eng in cons:
            self.items[eng].append(["wait", self.val])

    def replay(self, blk):
        sem = self.sem
        for eng in ENGS:
            if not self.items[eng]:
                continue

            def body(e, eng=eng):
                last_wait = -1
                for it in self.items[eng]:
                    if it[0] == "op":
                        it[1](e)
                    else:
                        if it[1] > last_wait:
                            e.wait_ge(sem, it[1])
                            last_wait = it[1]
            getattr(blk, eng)(body)



def _rope_tables():
    inv = 1.0 / (10000.0 ** (np.arange(0, DH, 2, dtype=np.float64) / DH))
    fr = np.arange(T, dtype=np.float64)[:, None] * inv[None, :]
    emb = np.concatenate([fr, fr], axis=-1)
    cos = np.cos(emb)
    sin = np.sin(emb)
    cosr = np.tile(cos, (1, H)).astype(BF)
    sgn = np.concatenate([-np.ones((T, DH // 2)), np.ones((T, DH // 2))], axis=-1)
    sinr = np.tile(sin * sgn, (1, H)).astype(BF)
    return cosr, sinr


def _rel_buckets_1d():
    rel = np.arange(-(T - 1), T, dtype=np.int64)
    nb = NBUCK // 2
    buckets = (rel > 0).astype(np.int64) * nb
    rp = np.abs(rel)
    max_exact = nb // 2
    is_small = rp < max_exact
    large = max_exact + (np.log(np.maximum(rp, 1) / max_exact)
                         / np.log(MAXD / max_exact) * (nb - max_exact)).astype(np.int64)
    large = np.minimum(large, nb - 1)
    return buckets + np.where(is_small, rp, large)


_BUCK = _rel_buckets_1d()


def _band_plan():
    mid = T - 1
    r_pos = mid + 1
    for r in range(1, T):
        if (_BUCK[mid + r:] == _BUCK[-1]).all():
            r_pos = r
            break
    r_neg = mid + 1
    for r in range(1, T):
        if (_BUCK[:mid - r + 1] == _BUCK[0]).all():
            r_neg = r
            break
    sides, tiles, slots, slotmap = [], [], [], {}
    for qi in range(NQB):
        i0 = qi * QB
        cls = []
        npos = nneg = 0
        for ki in range(KT):
            j0 = ki * 128
            lo, hi = j0 - i0 - (QB - 1), j0 + 127 - i0
            if hi <= -r_neg:
                cls.append("far_neg"); nneg += 1
            elif lo >= r_pos:
                cls.append("far_pos"); npos += 1
            else:
                cls.append("band")
        side = "pos" if npos >= nneg else "neg"
        sides.append(side)
        row = []
        for ki in range(KT):
            c = cls[ki]
            if (c == "far_pos" and side == "pos") or (c == "far_neg" and side == "neg"):
                row.append(("skip",))
            elif c in ("far_pos", "far_neg"):
                row.append(("rank1",))
            else:
                key = (side, ki * 128 - i0)
                if key not in slotmap:
                    slotmap[key] = len(slots)
                    slots.append(key)
                row.append(("band", slotmap[key]))
        tiles.append(row)
    return sides, tiles, slots


BAND_SIDES, BAND_TILES, BAND_SLOTS = _band_plan()
NSLOT = len(BAND_SLOTS)


def host_prep(inp):
    f = np.float32
    hc = {}
    hc["l0_wdk"] = np.ascontiguousarray(inp["L0_attn_w_dk"], f)
    hc["l0_wdv"] = np.ascontiguousarray(inp["L0_attn_w_dv"], f)
    hc["l0_wuk"] = np.ascontiguousarray(inp["L0_attn_w_uk"], f)
    hc["l0_wuv"] = np.ascontiguousarray(inp["L0_attn_w_uv"], f)
    hc["l0_wo"] = np.ascontiguousarray(inp["L0_attn_w_o"], f)
    hc["l0_bdk"] = np.asarray(inp["L0_attn_b_dk"], f).reshape(128, 1)
    hc["l0_bdv"] = np.asarray(inp["L0_attn_b_dv"], f).reshape(128, 1)
    hc["l0_buk"] = np.asarray(inp["L0_attn_b_uk"], f).reshape(1, 1024)
    hc["l0_buv"] = np.asarray(inp["L0_attn_b_uv"], f).reshape(1, 1024)
    hc["l0_bo"] = np.asarray(inp["L0_attn_b_o"], f).reshape(1, 1024)
    cosr, sinr = _rope_tables()
    hc["rope_cos"] = cosr
    hc["rope_sin"] = sinr

    hc["f_bwT"] = np.ascontiguousarray(np.asarray(inp["L0_ffn_base_W"], f).T)
    hc["f_bb"] = np.asarray(inp["L0_ffn_base_b"], f).reshape(1, 1024)
    hc["f_saT"] = np.ascontiguousarray(np.asarray(inp["L0_ffn_shared_A"], f).T)
    hc["f_sbT"] = np.ascontiguousarray(np.asarray(inp["L0_ffn_shared_B"], f).T)
    ra = np.asarray(inp["L0_ffn_routed_A"], f)
    hc["f_raT"] = np.ascontiguousarray(ra.reshape(96, 1024).T)
    rb = np.asarray(inp["L0_ffn_routed_B"], f)
    hc["f_rb"] = np.ascontiguousarray(rb.transpose(0, 2, 1).reshape(96, 1024))
    hc["f_rwT"] = np.ascontiguousarray(np.asarray(inp["L0_ffn_router_W"], f).T)
    hc["f_rbias"] = np.asarray(inp["L0_ffn_router_b"], f).reshape(1, 6)
    esel = np.zeros((6, 96), dtype=BF)
    for e in range(6):
        esel[e, e * 16:(e + 1) * 16] = 1
    hc["f_esel"] = esel
    hc["ident"] = np.eye(128).astype(BF)
    hc["jrev"] = np.eye(128)[::-1].copy().astype(BF)

    hc["a_wq"] = np.ascontiguousarray(inp["L1_attn_Wq"], f)
    hc["a_wk"] = np.ascontiguousarray(inp["L1_attn_Wk"], f)
    hc["a_wv"] = np.ascontiguousarray(inp["L1_attn_Wv"], f)
    hc["a_wo"] = np.ascontiguousarray(inp["L1_attn_Wo"], f)
    hc["a_bq"] = np.asarray(inp["L1_attn_bq"], f).reshape(1024, 1)
    hc["a_bk"] = np.asarray(inp["L1_attn_bk"], f).reshape(1024, 1)
    hc["a_bv"] = np.asarray(inp["L1_attn_bv"], f).reshape(1, 1024)
    hc["a_bo"] = np.asarray(inp["L1_attn_bo"], f).reshape(1, 1024)
    re_ = np.asarray(inp["L1_attn_rel_embed"], f)
    ttab = re_[_BUCK, :].T
    hc["a_trev"] = np.ascontiguousarray(ttab[:, ::-1])
    hc["a_cneg_col"] = np.ascontiguousarray(ttab[:, 0].reshape(16, 1))
    hc["a_cpos_col"] = np.ascontiguousarray(ttab[:, -1].reshape(16, 1))
    hc["a_cneg_row"] = np.ascontiguousarray(ttab[:, 0].reshape(1, 16))
    hc["a_cpos_row"] = np.ascontiguousarray(ttab[:, -1].reshape(1, 16))
    gw = np.asarray(inp["L1_attn_gru_w"], f)
    bd = np.zeros((128, 16), dtype=f)
    bd[0:64, 0:8] = gw
    bd[64:128, 8:16] = gw
    hc["a_gruwbd"] = bd
    hc["a_grub"] = np.asarray(inp["L1_attn_gru_b"], f).reshape(8, 1)
    hc["a_gc"] = np.asarray(inp["L1_attn_gru_const"], f).reshape(1, 16)
    pc = np.zeros((16, 4), dtype=BF)
    pc[0:4, 0] = 1; pc[4:8, 1] = 1; pc[8:12, 2] = 1; pc[12:16, 3] = 1
    hc["a_pairc"] = pc

    hc["r_wrT"] = np.ascontiguousarray(np.asarray(inp["L1_ffn_W_r"], f).T)
    hc["r_wkT"] = np.ascontiguousarray(np.asarray(inp["L1_ffn_W_k"], f).T)
    hc["r_wvT"] = np.ascontiguousarray(np.asarray(inp["L1_ffn_W_v"], f).T)
    hc["r_mur"] = np.asarray(inp["L1_ffn_mu_r"], f).reshape(1024, 1)
    hc["r_muk"] = np.asarray(inp["L1_ffn_mu_k"], f).reshape(1024, 1)
    return hc


STAGE_KEYS = {
    "l0a": ["l0_wdk", "l0_wdv", "l0_wuk", "l0_wuv", "l0_wo", "l0_bdk", "l0_bdv",
            "l0_buk", "l0_buv", "l0_bo", "rope_cos", "rope_sin",
            "jrev", "ident", "a_pairc"],
    "l0f": ["f_bwT", "f_bb", "f_saT", "f_sbT", "f_raT", "f_rb", "f_rwT", "f_rbias",
            "f_esel", "jrev", "ident", "a_pairc"],
    "l1a": ["a_wq", "a_wk", "a_wv", "a_wo", "a_bq", "a_bk", "a_bv", "a_bo",
            "a_trev", "a_cneg_col", "a_cpos_col", "a_cneg_row", "a_cpos_row",
            "a_gruwbd", "a_grub", "a_gc", "a_pairc", "jrev", "ident"],
    "l1f": ["r_wrT", "r_wkT", "r_wvT", "r_mur", "r_muk", "jrev", "ident", "a_pairc"],
}



class Ctx:
    pass


def make_ctx(nc):
    cx = Ctx()
    cx.nc = nc
    cx.ctx = ExitStack()
    sb = lambda n, sh, dt: cx.ctx.enter_context(nc.sbuf_tensor(n, sh, dt))
    ps = lambda n, sh, dt: cx.ctx.enter_context(nc.psum_tensor(n, sh, dt))
    cx.bigA = sb("bigA", [128, 16384], BF16)
    cx.bigC = sb("bigC", [128, 16384], BF16)
    cx.bigD = sb("bigD", [128, 16384], BF16)
    cx.vh = sb("vh", [128, 16640], BF16)
    cx.fst = sb("fst", [128, 4096], F32)
    cx.wbf = sb("wbf", [128, 8192], BF16)
    cx.ws = sb("ws", [128, 3328], BF16)
    cx.rtmp = sb("rtmp", [128, 2048], BF16)
    cx.aux = sb("aux", [128, max(NSLOT, 4) * 512], BF16)
    cx.ptile = sb("ptile", [128, 1024], BF16)
    cx.gt = sb("gt", [128, 512], BF16)
    cx.biasb = sb("biasb", [128, 1024], BF16)
    cx.rowt = sb("rowt", [1, 512], BF16)
    cx.rows = sb("rows", [1, 3072], BF16)
    cx.gall = sb("gall", [16, 2048], BF16)
    cx.growb = sb("growb", [1, 2048], BF16)
    cx.scol = sb("scol", [128, 12], F32)
    cx.sbias = sb("sbias", [128, 80], F32)
    cx.onesf = sb("onesf", [1, 128], F32)
    cx.onesb = sb("onesb", [1, 128], BF16)
    cx.jrev = sb("c_jrev", [128, 128], BF16)
    cx.ident = sb("c_ident", [128, 128], BF16)
    cx.pairc = sb("c_pairc", [16, 4], BF16)
    cx.gsb = sb("gsb", [16, 512], BF16)
    cx.msk6 = cx.ctx.enter_context(nc.sbuf_tensor("msk6", [128, 6], U32))
    cx.neg6 = sb("neg6", [128, 6], F32)
    cx.zero6 = sb("zero6", [128, 6], F32)
    cx.P0 = ps("P0", [128, 1024], F32)
    cx.P2 = ps("P2", [128, 512], F32)
    cx.P3 = ps("P3", [128, 512], F32)
    cx.PA = ps("PA", [128, 512], F32)
    cx.PB = ps("PB", [128, 512], F32)
    cx.PT = ps("PT", [128, 128], BF16)
    cx.sem = cx.ctx.enter_context(nc.semaphore("S"))
    cx.em = Em(cx.sem)
    return cx


def f0(cx): return cx.fst[:, 0:1024]
def f1(cx): return cx.fst[:, 1024:2048]
def f2(cx): return cx.fst[:, 2048:3072]
def f3(cx): return cx.fst[:, 3072:4096]


def PBUFS(cx):
    return [cx.P0[:, 0:512], cx.P0[:, 512:1024], cx.P2[:, :], cx.P3[:, :]]


def load_const_setup(cx, d):
    em = cx.em
    em.op("vector", lambda e: e.memset(cx.onesf[:, :], 1.0))
    em.op("vector", lambda e: e.memset(cx.onesb[:, :], 1.0))
    em.op("vector", lambda e: e.memset(cx.neg6[:, :], -1e30))
    em.op("vector", lambda e: e.memset(cx.zero6[:, :], 0.0))
    em.op("vector", lambda e: e.memset(cx.scol[:, 8:9], 1e-5))
    em.dma("sync", lambda e: e.dma_start(out=cx.jrev[:, :], in_=d["jrev"]))
    em.dma("sync", lambda e: e.dma_start(out=cx.ident[:, :], in_=d["ident"]))
    em.dma("sync", lambda e: e.dma_start(out=cx.pairc[:, :], in_=d["a_pairc"]))
    em.barrier()


def load_weight_bf16(cx, dram_ap, dst, dst_off, rows, cols):
    em = cx.em
    for i in range(rows // 128):
        em.dma("sync", lambda e, i=i: e.dma_start(
            out=f0(cx)[:, 0:cols], in_=dram_ap[i * 128:(i + 1) * 128, 0:cols]))
        em.barrier(cons=("vector",))
        o = dst_off + i * cols
        em.op("vector", lambda e, o=o, cols=cols: e.tensor_copy(
            out=dst[:, o:o + cols], in_=f0(cx)[:, 0:cols]))
        em.barrier(cons=("sync",))
    em.barrier()


def load_bias_row(cx, dram_ap, dst_off, n):
    em = cx.em
    em.dma("sync", lambda e: e.dma_start(out=f0(cx)[0:1, 0:n], in_=dram_ap))
    em.barrier(cons=("vector",))
    em.op("vector", lambda e: e.tensor_copy(out=cx.rows[0:1, dst_off:dst_off + n],
                                            in_=f0(cx)[0:1, 0:n]))
    em.barrier()


def ln_stats_apply(cx, src, out_ap, extra_add=None):
    em = cx.em
    s1 = cx.scol[:, 0:1]; s2 = cx.scol[:, 1:2]; mcol = cx.scol[:, 2:3]
    vcol = cx.scol[:, 3:4]; v2 = cx.scol[:, 4:5]; sd = cx.scol[:, 5:6]
    rcol = cx.scol[:, 6:7]; msq = cx.scol[:, 7:8]
    em.op("vector", lambda e: e.reduce_sum(s1, src, axis=AX.X))
    em.op("scalar", lambda e: e.activation(out=f2(cx), in_=src, func=AF.Square,
                                           accum_out=s2))
    em.barrier(cons=("vector",))
    em.op("vector", lambda e: e.tensor_scalar_mul(mcol, s1, 1.0 / D))
    em.op("vector", lambda e: e.tensor_scalar_mul(vcol, s2, 1.0 / D))
    em.barrier(cons=("vector",))
    em.op("vector", lambda e: e.tensor_mul(msq, mcol, mcol))
    em.barrier(cons=("vector",))
    em.op("vector", lambda e: e.tensor_sub(v2, vcol, msq))
    em.barrier(cons=("scalar",))
    em.op("scalar", lambda e: e.activation(out=sd, in_=v2, func=AF.Sqrt, bias=cx.scol[:, 8:9]))
    em.barrier(cons=("vector",))
    em.op("vector", lambda e: e.reciprocal(rcol, sd))
    em.barrier(cons=("vector",))
    if extra_add is None:
        em.op("vector", lambda e: e.tensor_scalar(
            out=out_ap, in0=src, scalar1=mcol, scalar2=rcol,
            op0=OP.subtract, op1=OP.mult))
        em.barrier()
    else:
        em.op("vector", lambda e: e.tensor_scalar(
            out=f2(cx), in0=src, scalar1=mcol, scalar2=rcol,
            op0=OP.subtract, op1=OP.mult))
        em.barrier(cons=("vector",))
        em.op("vector", lambda e: e.tensor_add(out_ap, f2(cx), extra_add))
        em.barrier()


def ln_pass(cx, x_dram, b, dst, dst_off_fn):
    em = cx.em
    slots = [f0(cx), f3(cx)]
    em.dma("sync", lambda e: e.dma_start(out=slots[0], in_=x_dram[b * T:b * T + 128, :]))
    em.barrier()
    for tt in range(TT):
        x = slots[tt % 2]
        if tt + 1 < TT:
            em.dma("sync", lambda e, tt=tt: e.dma_start(
                out=slots[(tt + 1) % 2],
                in_=x_dram[b * T + (tt + 1) * 128:b * T + (tt + 2) * 128, :]))
        o = dst_off_fn(tt)
        ln_stats_apply(cx, x, dst[:, o:o + 1024])


def transpose_pass(cx, src, src_off_fn, dst, dst_off_fn):
    em = cx.em
    q = 0
    for tt in range(TT):
        for ch in range(NCH):
            so = src_off_fn(tt, ch)
            do = dst_off_fn(tt, ch)
            eng = DMAQS[q % len(DMAQS)]
            q += 1
            em.dma(eng, lambda e, so=so, do=do: e.dma_start_transpose(
                dst[:, do:do + 128], src[:, so:so + 128]))
    em.barrier()


def postln_tile(cx, b, tt, src_f32, xin_dram, xout_dram):
    em = cx.em
    r0 = b * T + tt * 128
    em.dma("sync", lambda e: e.dma_start(out=f0(cx), in_=xin_dram[r0:r0 + 128, :]))
    ln_stats_apply(cx, src_f32, f1(cx), extra_add=f0(cx))
    em.dma("sync", lambda e: e.dma_start(out=xout_dram[r0:r0 + 128, :], in_=f1(cx)))
    em.barrier()



def emit_l0_attn(cx, d, xin, xout):
    em = cx.em
    load_const_setup(cx, d)
    load_bias_row(cx, d["l0_buk"], 0, 1024)
    load_bias_row(cx, d["l0_buv"], 1024, 1024)
    em.dma("sync", lambda e: e.dma_start(out=cx.sbias[:, 0:1], in_=d["l0_bdk"]))
    em.dma("sync", lambda e: e.dma_start(out=cx.sbias[:, 1:2], in_=d["l0_bdv"]))
    em.barrier()
    ckv = cx.wbf[:, 4096:8192]

    for b in range(NB):
        load_weight_bf16(cx, d["l0_wdk"], cx.wbf, 0, 1024, 128)
        load_weight_bf16(cx, d["l0_wdv"], cx.wbf, 1024, 1024, 128)
        load_weight_bf16(cx, d["l0_wuk"], cx.wbf, 2048, 128, 1024)
        load_weight_bf16(cx, d["l0_wuv"], cx.wbf, 3072, 128, 1024)
        ln_pass(cx, xin, b, cx.bigA, lambda tt: tt * 1024)
        if b == 0:
            dbg(cx, "xln_tm", cx.bigA, 16384)
        transpose_pass(cx, cx.bigA, lambda tt, ch: tt * 1024 + ch * 128,
                       cx.bigC, lambda tt, ch: ch * 2048 + tt * 128)
        if b == 0:
            dbg(cx, "xln_fm", cx.bigC, 16384)
        for cc in range(2):
            for ic in range(NCH):
                for tb in range(NQB):
                    pb = PBUFS(cx)[tb]
                    em.op("tensor", lambda e, cc=cc, ic=ic, tb=tb, pb=pb: e.matmul(
                        out=pb,
                        lhsT=cx.wbf[:, cc * 1024 + ic * 128:cc * 1024 + ic * 128 + 128],
                        rhs=cx.bigC[:, ic * 2048 + tb * 512:ic * 2048 + tb * 512 + 512],
                        start=(ic == 0), stop=(ic == NCH - 1), skip_group_check=True))
            em.barrier(cons=("vector",))
            for tb in range(NQB):
                pb = PBUFS(cx)[tb]
                em.op("vector", lambda e, cc=cc, tb=tb, pb=pb: e.tensor_scalar_add(
                    ckv[:, cc * 2048 + tb * 512:cc * 2048 + tb * 512 + 512],
                    pb, cx.sbias[:, cc:cc + 1]))
            em.barrier()
        for tt in range(TT):
            for ob in range(2):
                pb = PBUFS(cx)[ob]
                em.op("tensor", lambda e, tt=tt, ob=ob, pb=pb: e.matmul(
                    out=pb, lhsT=ckv[:, tt * 128:tt * 128 + 128],
                    rhs=cx.wbf[:, 2048 + ob * 512:2048 + ob * 512 + 512],
                    start=True, stop=False, skip_group_check=True))
                em.op("tensor", lambda e, ob=ob, pb=pb: e.matmul(
                    out=pb, lhsT=cx.onesb[0:1, :],
                    rhs=cx.rows[0:1, ob * 512:ob * 512 + 512],
                    start=False, stop=True, skip_group_check=True))
            em.barrier(cons=("scalar",))
            for ob in range(2):
                pb = PBUFS(cx)[ob]
                em.op("scalar", lambda e, tt=tt, ob=ob, pb=pb: e.activation(
                    out=cx.vh[:, tt * 1024 + ob * 512:tt * 1024 + ob * 512 + 512],
                    in_=pb, func=AF.Copy))
            em.barrier()
        if b == 0:
            dbg(cx, "ckv", ckv, 4096)
            dbg(cx, "k_tm", cx.vh, 16384)
        for tt in range(TT):
            em.dma("sync", lambda e, tt=tt: e.dma_start(
                out=cx.aux[:, 0:1024], in_=d["rope_cos"][tt * 128:tt * 128 + 128, :]))
            em.dma("sync", lambda e, tt=tt: e.dma_start(
                out=cx.aux[:, 1024:2048], in_=d["rope_sin"][tt * 128:tt * 128 + 128, :]))
            em.barrier(cons=("vector",))
            for (buf, off) in ((cx.bigA, tt * 1024), (cx.vh, tt * 1024)):
                qv = buf[:, off:off + 1024].rearrange("p (h t s) -> p h t s", t=2, s=32)
                sv = cx.aux[:, 1024:2048].rearrange("p (h t s) -> p h t s", t=2, s=32)
                t1 = cx.rtmp[:, 0:1024]
                t2 = cx.rtmp[:, 1024:2048]
                t2v = t2.rearrange("p (h t s) -> p h t s", t=2, s=32)
                em.op("vector", lambda e, buf=buf, off=off, t1=t1: e.tensor_mul(
                    t1, buf[:, off:off + 1024], cx.aux[:, 0:1024]))
                em.op("vector", lambda e, qv=qv, sv=sv, t2v=t2v: e.tensor_mul(
                    t2v[:, :, 0, :], qv[:, :, 1, :], sv[:, :, 0, :]))
                em.op("vector", lambda e, qv=qv, sv=sv, t2v=t2v: e.tensor_mul(
                    t2v[:, :, 1, :], qv[:, :, 0, :], sv[:, :, 1, :]))
                em.barrier(cons=("vector",))
                em.op("vector", lambda e, buf=buf, off=off, t1=t1, t2=t2: e.tensor_add(
                    buf[:, off:off + 1024], t1, t2))
                em.barrier(cons=("vector",))
            em.barrier()
        if b == 0:
            dbg(cx, "qr_tm", cx.bigA, 16384)
            dbg(cx, "kr_tm", cx.vh, 16384)
        transpose_pass(cx, cx.bigA, lambda tt, ch: tt * 1024 + ch * 128,
                       cx.bigD, lambda tt, ch: ch * 2048 + tt * 128)
        transpose_pass(cx, cx.vh, lambda tt, ch: tt * 1024 + ch * 128,
                       cx.bigC, lambda tt, ch: ch * 2048 + tt * 128)
        em.op("vector", lambda e: e.memset(cx.vh[:, :], 1.0))
        em.barrier()
        for tt in range(TT):
            for ob in range(2):
                pb = PBUFS(cx)[ob]
                em.op("tensor", lambda e, tt=tt, ob=ob, pb=pb: e.matmul(
                    out=pb, lhsT=ckv[:, 2048 + tt * 128:2048 + tt * 128 + 128],
                    rhs=cx.wbf[:, 3072 + ob * 512:3072 + ob * 512 + 512],
                    start=True, stop=False, skip_group_check=True))
                em.op("tensor", lambda e, ob=ob, pb=pb: e.matmul(
                    out=pb, lhsT=cx.onesb[0:1, :],
                    rhs=cx.rows[0:1, 1024 + ob * 512:1024 + ob * 512 + 512],
                    start=False, stop=True, skip_group_check=True))
            em.barrier(cons=("scalar",))
            for ob in range(2):
                pb = PBUFS(cx)[ob]
                dst = cx.vh[:, tt * 1040 + ob * 520:tt * 1040 + (ob + 1) * 520]
                dv = dst.rearrange("p (h x) -> p h x", x=65)[:, :, 0:64]
                pv = pb.rearrange("p (h x) -> p h x", x=64)
                em.op("scalar", lambda e, dv=dv, pv=pv: e.activation(
                    out=dv, in_=pv, func=AF.Copy))
            em.barrier()
        if b == 0:
            dbg(cx, "vaug", cx.vh, 16640)
            dbg(cx, "qr_fm", cx.bigD, 16384)
            dbg(cx, "kr_fm", cx.bigC, 16384)
        attention(cx, kbuf=cx.bigC, with_bias=False)
        if b == 0:
            dbg(cx, "attn", cx.bigD, 16384)
        load_weight_bf16(cx, d["l0_wo"], cx.wbf, 0, 1024, 1024)
        load_bias_row(cx, d["l0_bo"], 2048, 1024)
        proj_postln(cx, b, xin, xout, brow_off=2048)
    em.barrier(cons=("sync",))


def attention(cx, kbuf, with_bias):
    em = cx.em
    for h in range(H):
        r0 = 64 * (h % 2)
        chq = (h // 2) * 2048
        if with_bias:
            em.dma("sync", lambda e, h=h: e.dma_start(out=cx.growb[0:1, :],
                                                      in_=cx.gall[h:h + 1, :]))
            if "tfl" not in SKIPS:
                for s, (side, delta) in enumerate(BAND_SLOTS):
                    si = 0 if side == "neg" else 1
                    base = (si * 16 + h) * 4095 + (1920 - delta)
                    win = RawAP(cx.tsrev_t, base, [[1, 128], [1, 512]])
                    em.dma("sync", lambda e, s=s, win=win: e.dma_start(
                        out=cx.aux[:, s * 512:s * 512 + 512], in_=win))
            em.barrier()
        tok_out_prev = None
        for qi in range(NQB):
            qoff = chq + qi * 512
            if with_bias:
                side = BAND_SIDES[qi]
                if any(t[0] == "rank1" for t in BAND_TILES[qi]):
                    dd = (cx.sbias[0:1, 24 + h:24 + h + 1] if side == "pos"
                          else cx.sbias[0:1, 40 + h:40 + h + 1])
                    em.op("vector", lambda e, qi=qi, dd=dd: e.tensor_scalar(
                        out=cx.rowt[0:1, :],
                        in0=cx.growb[0:1, qi * 512:qi * 512 + 512],
                        scalar1=dd, scalar2=None, op0=OP.mult))
                    em.barrier(cons=("tensor",))
                if any(t[0] == "band" for t in BAND_TILES[qi]):
                    em.op("tensor", lambda e, qi=qi: e.matmul(
                        out=cx.PB[:, :], lhsT=cx.onesb[0:1, :],
                        rhs=cx.growb[0:1, qi * 512:qi * 512 + 512],
                        start=True, stop=True, skip_group_check=True))
                    em.barrier(cons=("vector",))
                    em.op("vector", lambda e: e.tensor_copy(out=cx.gt[:, :],
                                                            in_=cx.PB[:, :]))
                    em.barrier(cons=("vector", "tensor"))
            toks_exp = {}
            toks_out = {}
            tEs = {}
            for kt in range(KT):
                half = kt % 2
                pS = cx.P0[:, half * 512:half * 512 + 512]
                t = BAND_TILES[qi][kt] if with_bias else ("skip",)
                if t[0] == "band" and "band" in SKIPS:
                    t = ("skip",)
                if t[0] == "rank1" and "rank1" in SKIPS:
                    t = ("skip",)
                if kt >= 2:
                    em.wait("tensor", toks_exp[kt - 2])
                if kt == 0 and tok_out_prev is not None:
                    em.wait("tensor", tok_out_prev)
                nojmm = "jmm" in SKIPS
                if t[0] == "band":
                    bb = cx.biasb[:, half * 512:half * 512 + 512]
                    em.barrier(cons=("vector",))
                    em.op("vector", lambda e, t=t, bb=bb: e.tensor_mul(
                        bb, cx.aux[:, t[1] * 512:t[1] * 512 + 512], cx.gt[:, :]))
                    tv = em.tok("vector")
                    em.wait("tensor", tv)
                em.op("tensor", lambda e, kt=kt, pS=pS, qoff=qoff, r0=r0, chq=chq,
                      last=(t[0] == "skip" or (t[0] == "band" and nojmm)),
                      kb=kbuf: e.matmul(
                    out=pS, lhsT=kb[r0:r0 + 64, chq + kt * 128:chq + kt * 128 + 128],
                    rhs=cx.bigD[r0:r0 + 64, qoff:qoff + 512],
                    start=True, stop=last, skip_group_check=True))
                if t[0] == "rank1":
                    em.op("tensor", lambda e, pS=pS: e.matmul(
                        out=pS, lhsT=cx.onesb[0:1, :], rhs=cx.rowt[0:1, :],
                        start=False, stop=True, skip_group_check=True))
                elif t[0] == "band" and not nojmm:
                    bb = cx.biasb[:, half * 512:half * 512 + 512]
                    em.op("tensor", lambda e, pS=pS, bb=bb: e.matmul(
                        out=pS, lhsT=cx.jrev[:, :], rhs=bb,
                        start=False, stop=True, skip_group_check=True))
                tEs[kt] = em.tok("tensor")
                em.wait("scalar", tEs[kt])
                if kt >= 2:
                    em.wait("scalar", toks_out[kt - 2])
                pt = cx.ptile[:, half * 512:half * 512 + 512]
                em.op("scalar", lambda e, pS=pS, pt=pt: e.activation(
                    out=pt, in_=pS, func=AF.Exp, scale=0.125))
                toks_exp[kt] = em.tok("scalar")
                em.wait("tensor", toks_exp[kt])
                em.op("tensor", lambda e, kt=kt, h=h, pt=pt: e.matmul(
                    out=cx.P2[0:65, :],
                    lhsT=cx.vh[:, kt * 1040 + h * 65:kt * 1040 + h * 65 + 65],
                    rhs=pt, start=(kt == 0), stop=(kt == KT - 1),
                    skip_group_check=True))
                toks_out[kt] = em.tok("tensor")
            em.wait("vector", toks_out[KT - 1])
            em.op("vector", lambda e: e.reciprocal(cx.fst[0:1, 1024:1536],
                                                   cx.P2[64:65, :]))
            em.barrier(cons=("scalar", "tensor"))
            em.op("scalar", lambda e: e.activation(out=cx.fst[0:64, 2048:2560],
                                                   in_=cx.P2[0:64, :], func=AF.Copy))
            em.op("tensor", lambda e: e.matmul(out=cx.P3[0:64, :],
                                               lhsT=cx.onesf[0:1, 0:64],
                                               rhs=cx.fst[0:1, 1024:1536],
                                               start=True, stop=True,
                                               skip_group_check=True))
            em.barrier(cons=("vector",))
            em.op("vector", lambda e, r0=r0, qoff=qoff: e.tensor_mul(
                cx.bigD[r0:r0 + 64, qoff:qoff + 512], cx.fst[0:64, 2048:2560],
                cx.P3[0:64, :]))
            tok_out_prev = em.tok("vector")
        em.barrier()


def proj_postln(cx, b, xin, xout, brow_off):
    em = cx.em
    for tt in range(TT):
        for ob in range(2):
            pb = PBUFS(cx)[ob]
            for ch in range(NCH):
                em.op("tensor", lambda e, tt=tt, ob=ob, ch=ch, pb=pb: e.matmul(
                    out=pb,
                    lhsT=cx.bigD[:, ch * 2048 + tt * 128:ch * 2048 + tt * 128 + 128],
                    rhs=cx.wbf[:, ch * 1024 + ob * 512:ch * 1024 + ob * 512 + 512],
                    start=(ch == 0), stop=False, skip_group_check=True))
            em.op("tensor", lambda e, ob=ob, pb=pb: e.matmul(
                out=pb, lhsT=cx.onesb[0:1, :],
                rhs=cx.rows[0:1, brow_off + ob * 512:brow_off + ob * 512 + 512],
                start=False, stop=True, skip_group_check=True))
        em.barrier(cons=("scalar",))
        for ob in range(2):
            pb = PBUFS(cx)[ob]
            em.op("scalar", lambda e, ob=ob, pb=pb: e.activation(
                out=f1(cx)[:, ob * 512:ob * 512 + 512], in_=pb, func=AF.Copy))
        em.barrier()
        if tt == 0 and b == 0:
            dbg(cx, "proj0", cx.fst, 2048)
        postln_tile(cx, b, tt, f1(cx), xin, xout)



def emit_l0_ffn(cx, d, xin, xout):
    em = cx.em
    load_const_setup(cx, d)
    load_weight_bf16(cx, d["f_bwT"], cx.wbf, 0, 1024, 1024)
    load_weight_bf16(cx, d["f_saT"], cx.ws, 0, 1024, 16)
    em.dma("sync", lambda e: e.dma_start(out=f0(cx)[0:16, :], in_=d["f_sbT"]))
    em.barrier(cons=("vector",))
    em.op("vector", lambda e: e.tensor_copy(out=cx.ws[0:16, 128:1152],
                                            in_=f0(cx)[0:16, :]))
    em.barrier()
    load_weight_bf16(cx, d["f_raT"], cx.ws, 1152, 1024, 96)
    em.dma("sync", lambda e: e.dma_start(out=f0(cx)[0:96, :], in_=d["f_rb"]))
    em.barrier(cons=("vector",))
    em.op("vector", lambda e: e.tensor_copy(out=cx.ws[0:96, 1920:2944],
                                            in_=f0(cx)[0:96, :]))
    em.barrier()
    load_weight_bf16(cx, d["f_rwT"], cx.ws, 2944, 1024, 6)
    em.dma("sync", lambda e: e.dma_start(out=cx.ws[0:6, 2992:3088], in_=d["f_esel"]))
    load_bias_row(cx, d["f_bb"], 0, 1024)
    load_bias_row(cx, d["f_rbias"], 2048, 6)
    em.barrier()

    for b in range(NB):
        ln_pass(cx, xin, b, cx.bigA, lambda tt: tt * 1024)
        transpose_pass(cx, cx.bigA, lambda tt, ch: tt * 1024 + ch * 128,
                       cx.bigC, lambda tt, ch: ch * 2048 + tt * 128)
        for tb in range(NQB):
            for ch in range(NCH):
                em.op("tensor", lambda e, tb=tb, ch=ch: e.matmul(
                    out=cx.PA[0:96, :],
                    lhsT=cx.ws[:, 1152 + ch * 96:1152 + ch * 96 + 96],
                    rhs=cx.bigC[:, ch * 2048 + tb * 512:ch * 2048 + tb * 512 + 512],
                    start=(ch == 0), stop=(ch == NCH - 1), skip_group_check=True))
            em.barrier(cons=("vector",))
            em.op("vector", lambda e, tb=tb: e.tensor_copy(
                out=cx.bigD[0:96, tb * 512:tb * 512 + 512], in_=cx.PA[0:96, :]))
            em.barrier()
        for tb in range(NQB):
            for ch in range(NCH):
                em.op("tensor", lambda e, tb=tb, ch=ch: e.matmul(
                    out=cx.PA[0:16, :], lhsT=cx.ws[:, ch * 16:ch * 16 + 16],
                    rhs=cx.bigC[:, ch * 2048 + tb * 512:ch * 2048 + tb * 512 + 512],
                    start=(ch == 0), stop=(ch == NCH - 1), skip_group_check=True))
            em.barrier(cons=("vector",))
            em.op("vector", lambda e, tb=tb: e.tensor_copy(
                out=cx.vh[0:16, tb * 512:tb * 512 + 512], in_=cx.PA[0:16, :]))
            em.barrier()
        for tt in range(TT):
            for ch in range(NCH):
                em.op("tensor", lambda e, tt=tt, ch=ch: e.matmul(
                    out=cx.PA[:, 0:6],
                    lhsT=cx.bigC[:, ch * 2048 + tt * 128:ch * 2048 + tt * 128 + 128],
                    rhs=cx.ws[:, 2944 + ch * 6:2944 + ch * 6 + 6],
                    start=(ch == 0), stop=False, skip_group_check=True))
            em.op("tensor", lambda e: e.matmul(
                out=cx.PA[:, 0:6], lhsT=cx.onesb[0:1, :], rhs=cx.rows[0:1, 2048:2054],
                start=False, stop=True, skip_group_check=True))
            em.barrier(cons=("vector",))
            g = f2(cx)[:, 0:6]
            logit = f2(cx)[:, 8:14]
            wv = f2(cx)[:, 16:22]
            tmp6 = f2(cx)[:, 24:30]
            m1 = cx.scol[:, 0:1]; m2 = cx.scol[:, 1:2]; ssum = cx.scol[:, 2:3]
            nm = cx.scol[:, 3:4]
            em.op("vector", lambda e: e.tensor_copy(out=logit, in_=cx.PA[:, 0:6]))
            em.barrier(cons=("vector",))
            em.op("vector", lambda e: e.reduce_max(m1, logit, axis=AX.X))
            em.barrier(cons=("vector",))
            em.op("vector", lambda e: e.tensor_scalar_mul(nm, m1, -1.0))
            em.barrier(cons=("scalar",))
            em.op("scalar", lambda e: e.activation(out=g, in_=logit, func=AF.Exp,
                                                   bias=nm))
            em.barrier(cons=("vector",))
            em.op("vector", lambda e: e.reduce_sum(ssum, g, axis=AX.X))
            em.barrier(cons=("vector",))
            em.op("vector", lambda e: e.reciprocal(m2, ssum))
            em.barrier(cons=("vector",))
            em.op("vector", lambda e: e.tensor_scalar_mul(g, g, m2))
            em.barrier(cons=("vector",))
            em.op("vector", lambda e: e.reduce_max(m1, g, axis=AX.X))
            em.barrier(cons=("vector",))
            em.op("vector", lambda e: e.tensor_scalar(out=cx.msk6[:, :], in0=g,
                                                      scalar1=m1, scalar2=None,
                                                      op0=OP.is_ge))
            em.barrier(cons=("vector",))
            em.op("vector", lambda e: e.select(tmp6, cx.msk6[:, :], cx.neg6[:, :], g))
            em.barrier(cons=("vector",))
            em.op("vector", lambda e: e.reduce_max(m2, tmp6, axis=AX.X))
            em.barrier(cons=("vector",))
            em.op("vector", lambda e: e.tensor_scalar(out=cx.msk6[:, :], in0=g,
                                                      scalar1=m2, scalar2=None,
                                                      op0=OP.is_ge))
            em.barrier(cons=("vector",))
            em.op("vector", lambda e: e.select(wv, cx.msk6[:, :], g, cx.zero6[:, :]))
            em.barrier(cons=("vector",))
            em.op("vector", lambda e: e.reduce_sum(ssum, wv, axis=AX.X))
            em.barrier(cons=("vector",))
            em.op("vector", lambda e: e.tensor_scalar_add(ssum, ssum, 1e-6))
            em.barrier(cons=("vector",))
            em.op("vector", lambda e: e.reciprocal(m2, ssum))
            em.barrier(cons=("vector",))
            em.op("vector", lambda e: e.memset(cx.rtmp[:, 0:128], 0.0))
            em.barrier(cons=("vector",))
            em.op("vector", lambda e: e.tensor_scalar(out=cx.rtmp[:, 0:6], in0=wv,
                                                      scalar1=m2, scalar2=None,
                                                      op0=OP.mult))
            em.barrier(cons=("tensor",))
            em.op("tensor", lambda e: e.transpose(out=cx.PT[:, :], in_=cx.rtmp[:, 0:128],
                                                  identity=cx.ident[:, :]))
            em.barrier(cons=("vector",))
            em.op("vector", lambda e: e.tensor_copy(out=cx.rtmp[:, 128:256],
                                                    in_=cx.PT[:, :]))
            em.barrier(cons=("tensor",))
            em.op("tensor", lambda e: e.matmul(out=cx.PB[0:96, 0:128],
                                               lhsT=cx.ws[0:6, 2992:3088],
                                               rhs=cx.rtmp[0:6, 128:256],
                                               start=True, stop=True,
                                               skip_group_check=True))
            em.barrier(cons=("vector",))
            em.op("vector", lambda e, tt=tt: e.tensor_mul(
                cx.rtmp[0:96, 256:384], cx.bigD[0:96, tt * 128:tt * 128 + 128],
                cx.PB[0:96, 0:128]))
            em.barrier(cons=("tensor",))
            for ob in range(2):
                pb = PBUFS(cx)[ob]
                for ch in range(NCH):
                    em.op("tensor", lambda e, tt=tt, ob=ob, ch=ch, pb=pb: e.matmul(
                        out=pb,
                        lhsT=cx.bigC[:, ch * 2048 + tt * 128:ch * 2048 + tt * 128 + 128],
                        rhs=cx.wbf[:, ch * 1024 + ob * 512:ch * 1024 + ob * 512 + 512],
                        start=(ch == 0), stop=False, skip_group_check=True))
                em.op("tensor", lambda e, tt=tt, ob=ob, pb=pb: e.matmul(
                    out=pb, lhsT=cx.vh[0:16, tt * 128:tt * 128 + 128],
                    rhs=cx.ws[0:16, 128 + ob * 512:128 + ob * 512 + 512],
                    start=False, stop=False, skip_group_check=True))
                em.op("tensor", lambda e, ob=ob, pb=pb: e.matmul(
                    out=pb, lhsT=cx.rtmp[0:96, 256:384],
                    rhs=cx.ws[0:96, 1920 + ob * 512:1920 + ob * 512 + 512],
                    start=False, stop=False, skip_group_check=True))
                em.op("tensor", lambda e, ob=ob, pb=pb: e.matmul(
                    out=pb, lhsT=cx.onesb[0:1, :],
                    rhs=cx.rows[0:1, ob * 512:ob * 512 + 512],
                    start=False, stop=True, skip_group_check=True))
            em.barrier(cons=("scalar",))
            for ob in range(2):
                pb = PBUFS(cx)[ob]
                em.op("scalar", lambda e, ob=ob, pb=pb: e.activation(
                    out=f1(cx)[:, ob * 512:ob * 512 + 512], in_=pb, func=AF.Copy))
            em.barrier()
            postln_tile(cx, b, tt, f1(cx), xin, xout)
    em.barrier(cons=("sync",))



def emit_l1_attn(cx, d, xin, xout):
    em = cx.em
    load_const_setup(cx, d)
    em.dma("sync", lambda e: e.dma_start(out=cx.sbias[0:16, 73:74], in_=d["a_cneg_col"]))
    em.dma("sync", lambda e: e.dma_start(out=cx.sbias[0:16, 74:75], in_=d["a_cpos_col"]))
    em.dma("sync", lambda e: e.dma_start(out=cx.sbias[0:8, 75:76], in_=d["a_grub"]))
    em.dma("sync", lambda e: e.dma_start(out=cx.sbias[8:16, 75:76], in_=d["a_grub"]))
    em.dma("sync", lambda e: e.dma_start(out=cx.sbias[0:1, 56:72], in_=d["a_gc"]))
    em.barrier()
    if "ts" not in SKIPS:
        build_ts_tables(cx, d)
    em.dma("sync", lambda e: e.dma_start(out=f0(cx)[:, 0:16], in_=d["a_gruwbd"]))
    em.barrier(cons=("vector",))
    em.op("vector", lambda e: e.tensor_copy(out=cx.ws[:, 0:16], in_=f0(cx)[:, 0:16]))
    em.barrier()

    for b in range(NB):
        ln_pass(cx, xin, b, cx.bigA, lambda tt: tt * 1024)
        transpose_pass(cx, cx.bigA, lambda tt, ch: tt * 1024 + ch * 128,
                       cx.bigC, lambda tt, ch: ch * 2048 + tt * 128)
        if "gates" not in SKIPS:
            gates(cx, d)
        else:
            em.op("vector", lambda e: e.memset(cx.gall[:, :], 1.0))
            em.barrier()
        load_weight_bf16(cx, d["a_wq"], cx.wbf, 0, 1024, 1024)
        qkv_fm(cx, d, "a_bq", cx.bigD)
        load_weight_bf16(cx, d["a_wk"], cx.wbf, 0, 1024, 1024)
        qkv_fm(cx, d, "a_bk", cx.bigA)
        load_weight_bf16(cx, d["a_wv"], cx.wbf, 0, 1024, 1024)
        load_bias_row(cx, d["a_bv"], 1024, 1024)
        em.op("vector", lambda e: e.memset(cx.vh[:, :], 1.0))
        em.barrier()
        for tt in range(TT):
            for ob in range(2):
                pb = PBUFS(cx)[ob]
                for ic in range(NCH):
                    em.op("tensor", lambda e, tt=tt, ob=ob, ic=ic, pb=pb: e.matmul(
                        out=pb,
                        lhsT=cx.bigC[:, ic * 2048 + tt * 128:ic * 2048 + tt * 128 + 128],
                        rhs=cx.wbf[:, ic * 1024 + ob * 512:ic * 1024 + ob * 512 + 512],
                        start=(ic == 0), stop=False, skip_group_check=True))
                em.op("tensor", lambda e, ob=ob, pb=pb: e.matmul(
                    out=pb, lhsT=cx.onesb[0:1, :],
                    rhs=cx.rows[0:1, 1024 + ob * 512:1024 + ob * 512 + 512],
                    start=False, stop=True, skip_group_check=True))
            em.barrier(cons=("scalar",))
            for ob in range(2):
                pb = PBUFS(cx)[ob]
                dst = cx.vh[:, tt * 1040 + ob * 520:tt * 1040 + (ob + 1) * 520]
                dv = dst.rearrange("p (h x) -> p h x", x=65)[:, :, 0:64]
                pv = pb.rearrange("p (h x) -> p h x", x=64)
                em.op("scalar", lambda e, dv=dv, pv=pv: e.activation(
                    out=dv, in_=pv, func=AF.Copy))
            em.barrier()
        attention(cx, kbuf=cx.bigA, with_bias=("bias" not in SKIPS))
        load_weight_bf16(cx, d["a_wo"], cx.wbf, 0, 1024, 1024)
        load_bias_row(cx, d["a_bo"], 2048, 1024)
        proj_postln(cx, b, xin, xout, brow_off=2048)
    em.barrier(cons=("sync",))


def build_ts_tables(cx, d):
    em = cx.em
    tr = cx.fst[0:16, 0:4095]
    em.dma("sync", lambda e: e.dma_start(out=tr, in_=d["a_trev"]))
    em.barrier()
    for si, col in ((0, 73), (1, 74)):
        em.op("vector", lambda e, col=col: e.tensor_scalar(
            out=cx.wbf[0:16, 0:4095], in0=tr, scalar1=cx.sbias[0:16, col:col + 1],
            scalar2=None, op0=OP.subtract))
        em.barrier(cons=("sync",))
        em.dma("sync", lambda e, si=si: e.dma_start(
            out=cx.tsrev_dram[si, :, :], in_=cx.wbf[0:16, 0:4095]))
        em.barrier()
    em.dma("sync", lambda e: e.dma_start(out=cx.fst[0:1, 0:16], in_=d["a_cneg_row"]))
    em.dma("sync", lambda e: e.dma_start(out=cx.fst[0:1, 16:32], in_=d["a_cpos_row"]))
    em.barrier()
    em.op("vector", lambda e: e.tensor_sub(cx.sbias[0:1, 24:40], cx.fst[0:1, 0:16],
                                           cx.fst[0:1, 16:32]))
    em.op("vector", lambda e: e.tensor_sub(cx.sbias[0:1, 40:56], cx.fst[0:1, 16:32],
                                           cx.fst[0:1, 0:16]))
    em.barrier()


def gates(cx, d):
    em = cx.em
    for ch in range(NCH):
        for tb in range(NQB):
            em.op("tensor", lambda e, ch=ch, tb=tb: e.matmul(
                out=cx.PA[0:16, :], lhsT=cx.ws[:, 0:16],
                rhs=cx.bigC[:, ch * 2048 + tb * 512:ch * 2048 + tb * 512 + 512],
                start=True, stop=True, skip_group_check=True))
            em.barrier(cons=("vector",))
            em.op("vector", lambda e: e.tensor_scalar_add(cx.gsb[:, :], cx.PA[0:16, :],
                                                          cx.sbias[0:16, 75:76]))
            em.barrier(cons=("tensor",))
            em.op("tensor", lambda e: e.matmul(out=cx.PB[0:4, :], lhsT=cx.pairc[:, :],
                                               rhs=cx.gsb[:, :], start=True, stop=True,
                                               skip_group_check=True))
            em.barrier(cons=("scalar",))
            em.op("scalar", lambda e: e.activation(out=cx.fst[0:4, 0:512],
                                                   in_=cx.PB[0:4, :], func=AF.Sigmoid))
            em.barrier(cons=("sync",))
            for j in range(4):
                em.dma("sync", lambda e, j=j: e.dma_start(
                    out=cx.fst[0:1, 1024 + j * 512:1024 + (j + 1) * 512],
                    in_=cx.fst[j:j + 1, 0:512]))
            em.barrier()
            for hh in range(2):
                h = 2 * ch + hh
                ga = cx.fst[0:1, 1024 + (2 * hh) * 512:1024 + (2 * hh) * 512 + 512]
                gb = cx.fst[0:1, 1024 + (2 * hh + 1) * 512:1024 + (2 * hh + 1) * 512 + 512]
                t2 = cx.fst[0:1, 3072:3584]
                t3 = cx.fst[0:1, 3584:4096]
                em.op("vector", lambda e, gb=gb, h=h, t2=t2: e.tensor_scalar(
                    out=t2, in0=gb, scalar1=cx.sbias[0:1, 56 + h:56 + h + 1],
                    scalar2=-1.0, op0=OP.mult, op1=OP.add))
                em.barrier(cons=("vector",))
                em.op("vector", lambda e, ga=ga, t2=t2, t3=t3: e.tensor_mul(t3, ga, t2))
                em.barrier(cons=("vector",))
                em.op("vector", lambda e, t3=t3: e.tensor_scalar(
                    out=cx.growb[0:1, 0:512], in0=t3, scalar1=2.0, scalar2=8.0,
                    op0=OP.add, op1=OP.mult))
                em.barrier(cons=("sync",))
                em.dma("sync", lambda e, h=h, tb=tb: e.dma_start(
                    out=cx.gall[h:h + 1, tb * 512:tb * 512 + 512],
                    in_=cx.growb[0:1, 0:512]))
                em.barrier()


def qkv_fm(cx, d, bias_key, dst):
    em = cx.em
    for oc in range(NCH):
        em.dma("sync", lambda e, oc=oc: e.dma_start(
            out=cx.sbias[:, 3:4], in_=d[bias_key][oc * 128:oc * 128 + 128, :]))
        em.barrier(cons=("tensor",))
        for tb in range(NQB):
            pb = PBUFS(cx)[tb]
            for ic in range(NCH):
                em.op("tensor", lambda e, oc=oc, tb=tb, ic=ic, pb=pb: e.matmul(
                    out=pb,
                    lhsT=cx.wbf[:, ic * 1024 + oc * 128:ic * 1024 + oc * 128 + 128],
                    rhs=cx.bigC[:, ic * 2048 + tb * 512:ic * 2048 + tb * 512 + 512],
                    start=(ic == 0), stop=(ic == NCH - 1), skip_group_check=True))
        em.barrier(cons=("vector",))
        for tb in range(NQB):
            pb = PBUFS(cx)[tb]
            em.op("vector", lambda e, oc=oc, tb=tb, pb=pb: e.tensor_scalar_add(
                dst[:, oc * 2048 + tb * 512:oc * 2048 + tb * 512 + 512],
                pb, cx.sbias[:, 3:4]))
        em.barrier()



def emit_l1_ffn(cx, d, xin, xout):
    em = cx.em
    load_const_setup(cx, d)
    for i in range(NCH):
        em.dma("sync", lambda e, i=i: e.dma_start(
            out=cx.sbias[:, 8 + i:9 + i], in_=d["r_mur"][i * 128:i * 128 + 128, :]))
        em.dma("sync", lambda e, i=i: e.dma_start(
            out=cx.sbias[:, 16 + i:17 + i], in_=d["r_muk"][i * 128:i * 128 + 128, :]))
    em.barrier()
    for b in range(NB):
        ln_pass(cx, xin, b, cx.bigA, lambda tt: tt * 1024)
        transpose_pass(cx, cx.bigA, lambda tt, ch: tt * 1024 + ch * 128,
                       cx.bigC, lambda tt, ch: ch * 2048 + tt * 128)
        mix(cx, cx.bigD, mu_off=8)
        load_weight_bf16(cx, d["r_wrT"], cx.wbf, 0, 1024, 1024)
        fm_matmul_act(cx, cx.bigD, cx.bigA, AF.Sigmoid)
        mix(cx, cx.bigD, mu_off=16)
        load_weight_bf16(cx, d["r_wkT"], cx.wbf, 0, 1024, 1024)
        fm_matmul_act(cx, cx.bigD, cx.vh, AF.Relu)
        for ch in range(NCH):
            em.op("scalar", lambda e, ch=ch: e.activation(
                out=cx.bigD[:, ch * 2048:ch * 2048 + 2048],
                in_=cx.vh[:, ch * 2048:ch * 2048 + 2048], func=AF.Square))
        em.barrier()
        transpose_pass(cx, cx.bigA, lambda tt, ch: ch * 2048 + tt * 128,
                       cx.bigC, lambda tt, ch: tt * 1024 + ch * 128)
        load_weight_bf16(cx, d["r_wvT"], cx.wbf, 0, 1024, 1024)
        for tt in range(TT):
            for ob in range(2):
                pb = PBUFS(cx)[ob]
                for ic in range(NCH):
                    em.op("tensor", lambda e, tt=tt, ob=ob, ic=ic, pb=pb: e.matmul(
                        out=pb,
                        lhsT=cx.bigD[:, ic * 2048 + tt * 128:ic * 2048 + tt * 128 + 128],
                        rhs=cx.wbf[:, ic * 1024 + ob * 512:ic * 1024 + ob * 512 + 512],
                        start=(ic == 0), stop=(ic == NCH - 1), skip_group_check=True))
            em.barrier(cons=("vector",))
            for ob in range(2):
                pb = PBUFS(cx)[ob]
                em.op("vector", lambda e, tt=tt, ob=ob, pb=pb: e.tensor_mul(
                    f1(cx)[:, ob * 512:ob * 512 + 512], pb,
                    cx.bigC[:, tt * 1024 + ob * 512:tt * 1024 + ob * 512 + 512]))
            em.barrier()
            postln_tile(cx, b, tt, f1(cx), xin, xout)
    em.barrier(cons=("sync",))


def mix(cx, dst, mu_off):
    em = cx.em
    for ch in range(NCH):
        o = ch * 2048
        dd = cx.rtmp[:, 0:2048]
        em.op("vector", lambda e, o=o, dd=dd: e.tensor_sub(
            dd[:, 1:2048], cx.bigC[:, o + 1:o + 2048], cx.bigC[:, o:o + 2047]))
        em.barrier(cons=("vector",))
        mu = cx.sbias[:, mu_off + ch:mu_off + ch + 1]
        em.op("vector", lambda e, o=o, dd=dd, mu=mu: e.scalar_tensor_tensor(
            out=dst[:, o + 1:o + 2048], in0=dd[:, 1:2048], scalar=mu,
            in1=cx.bigC[:, o:o + 2047], op0=OP.mult, op1=OP.add))
        em.op("vector", lambda e, o=o, mu=mu: e.tensor_scalar_mul(
            dst[:, o:o + 1], cx.bigC[:, o:o + 1], mu))
        em.barrier(cons=("vector",))
    em.barrier()


def fm_matmul_act(cx, src, dst, func):
    em = cx.em
    for oc in range(NCH):
        for tb in range(NQB):
            pb = PBUFS(cx)[tb]
            for ic in range(NCH):
                em.op("tensor", lambda e, oc=oc, tb=tb, ic=ic, pb=pb: e.matmul(
                    out=pb,
                    lhsT=cx.wbf[:, ic * 1024 + oc * 128:ic * 1024 + oc * 128 + 128],
                    rhs=src[:, ic * 2048 + tb * 512:ic * 2048 + tb * 512 + 512],
                    start=(ic == 0), stop=(ic == NCH - 1), skip_group_check=True))
        em.barrier(cons=("scalar",))
        for tb in range(NQB):
            pb = PBUFS(cx)[tb]
            em.op("scalar", lambda e, oc=oc, tb=tb, pb=pb, func=func: e.activation(
                out=dst[:, oc * 2048 + tb * 512:oc * 2048 + tb * 512 + 512],
                in_=pb, func=func))
        em.barrier()


import os
DBG_TAG = os.environ.get("KDBG", "")
SKIPS = set(os.environ.get("KSKIP", "").split(","))


def dbg(cx, tag, buf, ncols):
    if DBG_TAG != tag:
        return
    em = cx.em
    em.barrier()
    for part in range(0, ncols, 2048):
        w = min(2048, ncols - part)
        em.dma("gpsimd", lambda e, part=part, w=w: e.dma_start(
            out=cx.dbg_dram[:, part:part + w], in_=buf[:, part:part + w]))
    em.barrier()



STAGES = ("l0a", "l0f", "l1a", "l1f")
EMITTERS = {"l0a": emit_l0_attn, "l0f": emit_l0_ffn,
            "l1a": emit_l1_attn, "l1f": emit_l1_ffn}
_NC_CACHE = {}


def build_stage_nc(stage, hc_shapes):
    key = stage
    if key in _NC_CACHE:
        return _NC_CACHE[key]
    nc = bass.Bass("TRN2", target_bir_lowering=False)
    xin = nc.dram_tensor("xin", [NB * T, D], F32, kind="ExternalInput").ap()
    xout = nc.dram_tensor("xout", [NB * T, D], F32, kind="ExternalOutput").ap()
    d = {}
    for k in STAGE_KEYS[stage]:
        sh, dt = hc_shapes[k]
        d[k] = nc.dram_tensor(k, list(sh), dt, kind="ExternalInput").ap()
    cx = make_ctx(nc)
    cx.dbg_dram = nc.dram_tensor("dbgout", [128, 16640], BF16,
                                 kind="ExternalOutput").ap()
    if stage == "l1a":
        th = nc.dram_tensor("tsrev", [2, 16, 4095], BF16, kind="Internal")
        cx.tsrev_dram = th.ap()
        cx.tsrev_t = cx.tsrev_dram.tensor
    EMITTERS[stage](cx, d, xin, xout)
    with nc.Block() as blk:
        cx.em.replay(blk)
    cx.ctx.close()
    _NC_CACHE[key] = nc
    return nc


def kernel(**inputs):
    hc = host_prep(inputs)
    x = np.ascontiguousarray(np.asarray(inputs["L0_attn_x"], np.float32)
                             .reshape(16 * T, D))
    shards = [x[c * NB * T:(c + 1) * NB * T, :] for c in range(NCORE)]
    hc_shapes = {k: (v.shape, BF16 if v.dtype == np.dtype(BF) else F32)
                 for k, v in hc.items()}
    cur = shards
    for stage in STAGES:
        nc = build_stage_nc(stage, hc_shapes)
        in_maps = []
        for c in range(NCORE):
            m = {"xin": cur[c]}
            for k in STAGE_KEYS[stage]:
                m[k] = hc[k]
            in_maps.append(m)
        res = run_bass_kernel_spmd(nc, in_maps, core_ids=list(range(NCORE)))
        cur = [np.asarray(r["xout"], np.float32) for r in res.results]
    return np.concatenate(cur, axis=0).reshape(16, T, D)


# version 03
# speedup vs baseline: 1.0049x; 1.0049x over previous
import numpy as np
import ml_dtypes
from contextlib import ExitStack
import concourse.bass as bass
import concourse.mybir as mybir
from concourse.bass_utils import run_bass_kernel_spmd
from bass_rust import AP as RawAP

F32 = mybir.dt.float32
BF16 = mybir.dt.bfloat16
U32 = mybir.dt.uint32
AF = mybir.ActivationFunctionType
OP = mybir.AluOpType
AX = mybir.AxisListType
BF = ml_dtypes.bfloat16

NCORE = 8
NB = 2
T = 2048
D = 1024
H, DH = 16, 64
TT = T // 128
NCH = D // 128
QB = 512
NQB = T // QB
KT = T // 128
NBUCK, MAXD = 32, 128
ENGS = ("sync", "tensor", "vector", "scalar", "gpsimd")
DMAQS = ("sync",)



class Em:
    def __init__(self, sem):
        self.sem = sem
        self.items = {e: [] for e in ENGS}
        self.val = 0
        self.dirty = {}

    def op(self, eng, fn):
        self.items[eng].append(["op", fn])
        self.dirty[eng] = True

    def dma(self, eng, fn):
        sem = self.sem

        def wrapped(e, fn=fn):
            return fn(e).then_inc(sem, 16)
        self.items[eng].append(["op", wrapped])
        self.val += 16
        return self.val

    def tok(self, eng):
        sem = self.sem
        item = self.items[eng][-1]
        assert item[0] == "op"
        fn = item[1]

        def wrapped(e, fn=fn):
            return fn(e).then_inc(sem, 1)
        item[1] = wrapped
        self.dirty.pop(eng, None)
        self.val += 1
        return self.val

    def wait(self, eng, v):
        self.items[eng].append(["wait", self.sem, v])

    def barrier(self, cons=ENGS):
        for eng in list(self.dirty):
            self.tok(eng)
        for eng in cons:
            self.items[eng].append(["wait", self.sem, self.val])

    def new_epoch(self, sem):
        self.barrier()
        self.sem = sem
        self.val = 0

    def replay(self, blk):
        for eng in ENGS:
            if not self.items[eng]:
                continue

            def body(e, eng=eng):
                last = {}
                for it in self.items[eng]:
                    if it[0] == "op":
                        it[1](e)
                    else:
                        sem, v = it[1], it[2]
                        k = id(sem)
                        if last.get(k, -1) < v:
                            e.wait_ge(sem, v)
                            last[k] = v
            getattr(blk, eng)(body)



def _rope_tables():
    inv = 1.0 / (10000.0 ** (np.arange(0, DH, 2, dtype=np.float64) / DH))
    fr = np.arange(T, dtype=np.float64)[:, None] * inv[None, :]
    emb = np.concatenate([fr, fr], axis=-1)
    cos = np.cos(emb)
    sin = np.sin(emb)
    cosr = np.tile(cos, (1, H)).astype(BF)
    sgn = np.concatenate([-np.ones((T, DH // 2)), np.ones((T, DH // 2))], axis=-1)
    sinr = np.tile(sin * sgn, (1, H)).astype(BF)
    return cosr, sinr


def _rel_buckets_1d():
    rel = np.arange(-(T - 1), T, dtype=np.int64)
    nb = NBUCK // 2
    buckets = (rel > 0).astype(np.int64) * nb
    rp = np.abs(rel)
    max_exact = nb // 2
    is_small = rp < max_exact
    large = max_exact + (np.log(np.maximum(rp, 1) / max_exact)
                         / np.log(MAXD / max_exact) * (nb - max_exact)).astype(np.int64)
    large = np.minimum(large, nb - 1)
    return buckets + np.where(is_small, rp, large)


_BUCK = _rel_buckets_1d()


def _band_plan():
    mid = T - 1
    r_pos = mid + 1
    for r in range(1, T):
        if (_BUCK[mid + r:] == _BUCK[-1]).all():
            r_pos = r
            break
    r_neg = mid + 1
    for r in range(1, T):
        if (_BUCK[:mid - r + 1] == _BUCK[0]).all():
            r_neg = r
            break
    sides, tiles, slots, slotmap = [], [], [], {}
    for qi in range(NQB):
        i0 = qi * QB
        cls = []
        npos = nneg = 0
        for ki in range(KT):
            j0 = ki * 128
            lo, hi = j0 - i0 - (QB - 1), j0 + 127 - i0
            if hi <= -r_neg:
                cls.append("far_neg"); nneg += 1
            elif lo >= r_pos:
                cls.append("far_pos"); npos += 1
            else:
                cls.append("band")
        side = "pos" if npos >= nneg else "neg"
        sides.append(side)
        row = []
        for ki in range(KT):
            c = cls[ki]
            if (c == "far_pos" and side == "pos") or (c == "far_neg" and side == "neg"):
                row.append(("skip",))
            elif c in ("far_pos", "far_neg"):
                row.append(("rank1",))
            else:
                key = (side, ki * 128 - i0)
                if key not in slotmap:
                    slotmap[key] = len(slots)
                    slots.append(key)
                row.append(("band", slotmap[key]))
        tiles.append(row)
    return sides, tiles, slots


BAND_SIDES, BAND_TILES, BAND_SLOTS = _band_plan()
NSLOT = len(BAND_SLOTS)


def host_prep(inp):
    f = np.float32
    hc = {}
    hc["l0_wdk"] = np.ascontiguousarray(inp["L0_attn_w_dk"], f)
    hc["l0_wdv"] = np.ascontiguousarray(inp["L0_attn_w_dv"], f)
    hc["l0_wuk"] = np.ascontiguousarray(inp["L0_attn_w_uk"], f)
    hc["l0_wuv"] = np.ascontiguousarray(inp["L0_attn_w_uv"], f)
    hc["l0_wo"] = np.ascontiguousarray(inp["L0_attn_w_o"], f)
    hc["l0_bdk"] = np.asarray(inp["L0_attn_b_dk"], f).reshape(128, 1)
    hc["l0_bdv"] = np.asarray(inp["L0_attn_b_dv"], f).reshape(128, 1)
    hc["l0_buk"] = np.asarray(inp["L0_attn_b_uk"], f).reshape(1, 1024)
    hc["l0_buv"] = np.asarray(inp["L0_attn_b_uv"], f).reshape(1, 1024)
    hc["l0_bo"] = np.asarray(inp["L0_attn_b_o"], f).reshape(1, 1024)
    cosr, sinr = _rope_tables()
    hc["rope_cos"] = cosr
    hc["rope_sin"] = sinr

    hc["f_bwT"] = np.ascontiguousarray(np.asarray(inp["L0_ffn_base_W"], f).T)
    hc["f_bb"] = np.asarray(inp["L0_ffn_base_b"], f).reshape(1, 1024)
    hc["f_saT"] = np.ascontiguousarray(np.asarray(inp["L0_ffn_shared_A"], f).T)
    hc["f_sbT"] = np.ascontiguousarray(np.asarray(inp["L0_ffn_shared_B"], f).T)
    ra = np.asarray(inp["L0_ffn_routed_A"], f)
    hc["f_raT"] = np.ascontiguousarray(ra.reshape(96, 1024).T)
    rb = np.asarray(inp["L0_ffn_routed_B"], f)
    hc["f_rb"] = np.ascontiguousarray(rb.transpose(0, 2, 1).reshape(96, 1024))
    hc["f_rwT"] = np.ascontiguousarray(np.asarray(inp["L0_ffn_router_W"], f).T)
    hc["f_rbias"] = np.asarray(inp["L0_ffn_router_b"], f).reshape(1, 6)
    esel = np.zeros((6, 96), dtype=BF)
    for e in range(6):
        esel[e, e * 16:(e + 1) * 16] = 1
    hc["f_esel"] = esel
    hc["ident"] = np.eye(128).astype(BF)
    hc["jrev"] = np.eye(128)[::-1].copy().astype(BF)

    hc["a_wq"] = np.ascontiguousarray(inp["L1_attn_Wq"], f)
    hc["a_wk"] = np.ascontiguousarray(inp["L1_attn_Wk"], f)
    hc["a_wv"] = np.ascontiguousarray(inp["L1_attn_Wv"], f)
    hc["a_wo"] = np.ascontiguousarray(inp["L1_attn_Wo"], f)
    hc["a_bq"] = np.asarray(inp["L1_attn_bq"], f).reshape(1024, 1)
    hc["a_bk"] = np.asarray(inp["L1_attn_bk"], f).reshape(1024, 1)
    hc["a_bv"] = np.asarray(inp["L1_attn_bv"], f).reshape(1, 1024)
    hc["a_bo"] = np.asarray(inp["L1_attn_bo"], f).reshape(1, 1024)
    re_ = np.asarray(inp["L1_attn_rel_embed"], f)
    ttab = re_[_BUCK, :].T
    hc["a_trev"] = np.ascontiguousarray(ttab[:, ::-1])
    hc["a_cneg_col"] = np.ascontiguousarray(ttab[:, 0].reshape(16, 1))
    hc["a_cpos_col"] = np.ascontiguousarray(ttab[:, -1].reshape(16, 1))
    hc["a_cneg_row"] = np.ascontiguousarray(ttab[:, 0].reshape(1, 16))
    hc["a_cpos_row"] = np.ascontiguousarray(ttab[:, -1].reshape(1, 16))
    gw = np.asarray(inp["L1_attn_gru_w"], f)
    bd = np.zeros((128, 16), dtype=f)
    bd[0:64, 0:8] = gw
    bd[64:128, 8:16] = gw
    hc["a_gruwbd"] = bd
    hc["a_grub"] = np.asarray(inp["L1_attn_gru_b"], f).reshape(8, 1)
    hc["a_gc"] = np.asarray(inp["L1_attn_gru_const"], f).reshape(1, 16)
    pc = np.zeros((16, 4), dtype=BF)
    pc[0:4, 0] = 1; pc[4:8, 1] = 1; pc[8:12, 2] = 1; pc[12:16, 3] = 1
    hc["a_pairc"] = pc

    hc["r_wrT"] = np.ascontiguousarray(np.asarray(inp["L1_ffn_W_r"], f).T)
    hc["r_wkT"] = np.ascontiguousarray(np.asarray(inp["L1_ffn_W_k"], f).T)
    hc["r_wvT"] = np.ascontiguousarray(np.asarray(inp["L1_ffn_W_v"], f).T)
    hc["r_mur"] = np.asarray(inp["L1_ffn_mu_r"], f).reshape(1024, 1)
    hc["r_muk"] = np.asarray(inp["L1_ffn_mu_k"], f).reshape(1024, 1)
    return hc


STAGE_KEYS = {
    "l0a": ["l0_wdk", "l0_wdv", "l0_wuk", "l0_wuv", "l0_wo", "l0_bdk", "l0_bdv",
            "l0_buk", "l0_buv", "l0_bo", "rope_cos", "rope_sin",
            "jrev", "ident", "a_pairc"],
    "l0f": ["f_bwT", "f_bb", "f_saT", "f_sbT", "f_raT", "f_rb", "f_rwT", "f_rbias",
            "f_esel", "jrev", "ident", "a_pairc"],
    "l1a": ["a_wq", "a_wk", "a_wv", "a_wo", "a_bq", "a_bk", "a_bv", "a_bo",
            "a_trev", "a_cneg_col", "a_cpos_col", "a_cneg_row", "a_cpos_row",
            "a_gruwbd", "a_grub", "a_gc", "a_pairc", "jrev", "ident"],
    "l1f": ["r_wrT", "r_wkT", "r_wvT", "r_mur", "r_muk", "jrev", "ident", "a_pairc"],
}



class Ctx:
    pass


def make_ctx(nc):
    cx = Ctx()
    cx.nc = nc
    cx.ctx = ExitStack()
    sb = lambda n, sh, dt: cx.ctx.enter_context(nc.sbuf_tensor(n, sh, dt))
    ps = lambda n, sh, dt: cx.ctx.enter_context(nc.psum_tensor(n, sh, dt))
    cx.bigA = sb("bigA", [128, 16384], BF16)
    cx.bigC = sb("bigC", [128, 16384], BF16)
    cx.bigD = sb("bigD", [128, 16384], BF16)
    cx.vh = sb("vh", [128, 16640], BF16)
    cx.fst = sb("fst", [128, 4096], F32)
    cx.wbf = sb("wbf", [128, 8192], BF16)
    cx.ws = sb("ws", [128, 3328], BF16)
    cx.rtmp = sb("rtmp", [128, 2048], BF16)
    cx.aux = sb("aux", [128, max(NSLOT, 4) * 512], BF16)
    cx.ptile = sb("ptile", [128, 1024], BF16)
    cx.gt = sb("gt", [128, 512], BF16)
    cx.biasb = sb("biasb", [128, 1024], BF16)
    cx.rowt = sb("rowt", [1, 512], BF16)
    cx.rows = sb("rows", [1, 3072], BF16)
    cx.gall = sb("gall", [16, 2048], BF16)
    cx.growb = sb("growb", [1, 2048], BF16)
    cx.scol = sb("scol", [128, 12], F32)
    cx.sbias = sb("sbias", [128, 80], F32)
    cx.onesf = sb("onesf", [1, 128], F32)
    cx.onesb = sb("onesb", [1, 128], BF16)
    cx.jrev = sb("c_jrev", [128, 128], BF16)
    cx.ident = sb("c_ident", [128, 128], BF16)
    cx.pairc = sb("c_pairc", [16, 4], BF16)
    cx.gsb = sb("gsb", [16, 512], BF16)
    cx.msk6 = cx.ctx.enter_context(nc.sbuf_tensor("msk6", [128, 6], U32))
    cx.neg6 = sb("neg6", [128, 6], F32)
    cx.zero6 = sb("zero6", [128, 6], F32)
    cx.P0 = ps("P0", [128, 1024], F32)
    cx.P2 = ps("P2", [128, 512], F32)
    cx.P3 = ps("P3", [128, 512], F32)
    cx.PA = ps("PA", [128, 512], F32)
    cx.PB = ps("PB", [128, 512], F32)
    cx.PT = ps("PT", [128, 128], BF16)
    cx.sems = [cx.ctx.enter_context(nc.semaphore(f"S{i}")) for i in range(4)]
    cx.em = Em(cx.sems[0])
    return cx


def f0(cx): return cx.fst[:, 0:1024]
def f1(cx): return cx.fst[:, 1024:2048]
def f2(cx): return cx.fst[:, 2048:3072]
def f3(cx): return cx.fst[:, 3072:4096]


def PBUFS(cx):
    return [cx.P0[:, 0:512], cx.P0[:, 512:1024], cx.P2[:, :], cx.P3[:, :]]


def load_const_setup(cx, d):
    em = cx.em
    em.op("vector", lambda e: e.memset(cx.onesf[:, :], 1.0))
    em.op("vector", lambda e: e.memset(cx.onesb[:, :], 1.0))
    em.op("vector", lambda e: e.memset(cx.neg6[:, :], -1e30))
    em.op("vector", lambda e: e.memset(cx.zero6[:, :], 0.0))
    em.op("vector", lambda e: e.memset(cx.scol[:, 8:9], 1e-5))
    em.dma("sync", lambda e: e.dma_start(out=cx.jrev[:, :], in_=d["jrev"]))
    em.dma("sync", lambda e: e.dma_start(out=cx.ident[:, :], in_=d["ident"]))
    em.dma("sync", lambda e: e.dma_start(out=cx.pairc[:, :], in_=d["a_pairc"]))
    em.barrier()


def load_weight_bf16(cx, dram_ap, dst, dst_off, rows, cols):
    em = cx.em
    for i in range(rows // 128):
        em.dma("sync", lambda e, i=i: e.dma_start(
            out=f0(cx)[:, 0:cols], in_=dram_ap[i * 128:(i + 1) * 128, 0:cols]))
        em.barrier(cons=("vector",))
        o = dst_off + i * cols
        em.op("vector", lambda e, o=o, cols=cols: e.tensor_copy(
            out=dst[:, o:o + cols], in_=f0(cx)[:, 0:cols]))
        em.barrier(cons=("sync",))
    em.barrier()


def load_bias_row(cx, dram_ap, dst_off, n):
    em = cx.em
    em.dma("sync", lambda e: e.dma_start(out=f0(cx)[0:1, 0:n], in_=dram_ap))
    em.barrier(cons=("vector",))
    em.op("vector", lambda e: e.tensor_copy(out=cx.rows[0:1, dst_off:dst_off + n],
                                            in_=f0(cx)[0:1, 0:n]))
    em.barrier()


def ln_stats_apply(cx, src, out_ap, extra_add=None):
    em = cx.em
    s1 = cx.scol[:, 0:1]; s2 = cx.scol[:, 1:2]; mcol = cx.scol[:, 2:3]
    vcol = cx.scol[:, 3:4]; v2 = cx.scol[:, 4:5]; sd = cx.scol[:, 5:6]
    rcol = cx.scol[:, 6:7]; msq = cx.scol[:, 7:8]
    em.op("vector", lambda e: e.reduce_sum(s1, src, axis=AX.X))
    em.op("scalar", lambda e: e.activation(out=f2(cx), in_=src, func=AF.Square,
                                           accum_out=s2))
    em.barrier(cons=("vector",))
    em.op("vector", lambda e: e.tensor_scalar_mul(mcol, s1, 1.0 / D))
    em.op("vector", lambda e: e.tensor_scalar_mul(vcol, s2, 1.0 / D))
    em.barrier(cons=("vector",))
    em.op("vector", lambda e: e.tensor_mul(msq, mcol, mcol))
    em.barrier(cons=("vector",))
    em.op("vector", lambda e: e.tensor_sub(v2, vcol, msq))
    em.barrier(cons=("scalar",))
    em.op("scalar", lambda e: e.activation(out=sd, in_=v2, func=AF.Sqrt, bias=cx.scol[:, 8:9]))
    em.barrier(cons=("vector",))
    em.op("vector", lambda e: e.reciprocal(rcol, sd))
    em.barrier(cons=("vector",))
    if extra_add is None:
        em.op("vector", lambda e: e.tensor_scalar(
            out=out_ap, in0=src, scalar1=mcol, scalar2=rcol,
            op0=OP.subtract, op1=OP.mult))
        em.barrier()
    else:
        em.op("vector", lambda e: e.tensor_scalar(
            out=f2(cx), in0=src, scalar1=mcol, scalar2=rcol,
            op0=OP.subtract, op1=OP.mult))
        em.barrier(cons=("vector",))
        em.op("vector", lambda e: e.tensor_add(out_ap, f2(cx), extra_add))
        em.barrier()


def ln_pass(cx, x_dram, b, dst, dst_off_fn):
    em = cx.em
    slots = [f0(cx), f3(cx)]
    em.dma("sync", lambda e: e.dma_start(out=slots[0], in_=x_dram[b * T:b * T + 128, :]))
    em.barrier()
    for tt in range(TT):
        x = slots[tt % 2]
        if tt + 1 < TT:
            em.dma("sync", lambda e, tt=tt: e.dma_start(
                out=slots[(tt + 1) % 2],
                in_=x_dram[b * T + (tt + 1) * 128:b * T + (tt + 2) * 128, :]))
        o = dst_off_fn(tt)
        ln_stats_apply(cx, x, dst[:, o:o + 1024])


def transpose_pass(cx, src, src_off_fn, dst, dst_off_fn):
    em = cx.em
    q = 0
    for tt in range(TT):
        for ch in range(NCH):
            so = src_off_fn(tt, ch)
            do = dst_off_fn(tt, ch)
            eng = DMAQS[q % len(DMAQS)]
            q += 1
            em.dma(eng, lambda e, so=so, do=do: e.dma_start_transpose(
                dst[:, do:do + 128], src[:, so:so + 128]))
    em.barrier()


def postln_tile(cx, b, tt, src_f32, xin_dram, xout_dram):
    em = cx.em
    r0 = b * T + tt * 128
    em.dma("sync", lambda e: e.dma_start(out=f0(cx), in_=xin_dram[r0:r0 + 128, :]))
    ln_stats_apply(cx, src_f32, f1(cx), extra_add=f0(cx))
    em.dma("sync", lambda e: e.dma_start(out=xout_dram[r0:r0 + 128, :], in_=f1(cx)))
    em.barrier()



def emit_l0_attn(cx, d, xin, xout):
    em = cx.em
    load_const_setup(cx, d)
    load_bias_row(cx, d["l0_buk"], 0, 1024)
    load_bias_row(cx, d["l0_buv"], 1024, 1024)
    em.dma("sync", lambda e: e.dma_start(out=cx.sbias[:, 0:1], in_=d["l0_bdk"]))
    em.dma("sync", lambda e: e.dma_start(out=cx.sbias[:, 1:2], in_=d["l0_bdv"]))
    em.barrier()
    ckv = cx.wbf[:, 4096:8192]

    for b in range(NB):
        load_weight_bf16(cx, d["l0_wdk"], cx.wbf, 0, 1024, 128)
        load_weight_bf16(cx, d["l0_wdv"], cx.wbf, 1024, 1024, 128)
        load_weight_bf16(cx, d["l0_wuk"], cx.wbf, 2048, 128, 1024)
        load_weight_bf16(cx, d["l0_wuv"], cx.wbf, 3072, 128, 1024)
        ln_pass(cx, xin, b, cx.bigA, lambda tt: tt * 1024)
        if b == 0:
            dbg(cx, "xln_tm", cx.bigA, 16384)
        transpose_pass(cx, cx.bigA, lambda tt, ch: tt * 1024 + ch * 128,
                       cx.bigC, lambda tt, ch: ch * 2048 + tt * 128)
        if b == 0:
            dbg(cx, "xln_fm", cx.bigC, 16384)
        for cc in range(2):
            for ic in range(NCH):
                for tb in range(NQB):
                    pb = PBUFS(cx)[tb]
                    em.op("tensor", lambda e, cc=cc, ic=ic, tb=tb, pb=pb: e.matmul(
                        out=pb,
                        lhsT=cx.wbf[:, cc * 1024 + ic * 128:cc * 1024 + ic * 128 + 128],
                        rhs=cx.bigC[:, ic * 2048 + tb * 512:ic * 2048 + tb * 512 + 512],
                        start=(ic == 0), stop=(ic == NCH - 1), skip_group_check=True))
            em.barrier(cons=("vector",))
            for tb in range(NQB):
                pb = PBUFS(cx)[tb]
                em.op("vector", lambda e, cc=cc, tb=tb, pb=pb: e.tensor_scalar_add(
                    ckv[:, cc * 2048 + tb * 512:cc * 2048 + tb * 512 + 512],
                    pb, cx.sbias[:, cc:cc + 1]))
            em.barrier()
        for tt in range(TT):
            for ob in range(2):
                pb = PBUFS(cx)[ob]
                em.op("tensor", lambda e, tt=tt, ob=ob, pb=pb: e.matmul(
                    out=pb, lhsT=ckv[:, tt * 128:tt * 128 + 128],
                    rhs=cx.wbf[:, 2048 + ob * 512:2048 + ob * 512 + 512],
                    start=True, stop=False, skip_group_check=True))
                em.op("tensor", lambda e, ob=ob, pb=pb: e.matmul(
                    out=pb, lhsT=cx.onesb[0:1, :],
                    rhs=cx.rows[0:1, ob * 512:ob * 512 + 512],
                    start=False, stop=True, skip_group_check=True))
            em.barrier(cons=("scalar",))
            for ob in range(2):
                pb = PBUFS(cx)[ob]
                em.op("scalar", lambda e, tt=tt, ob=ob, pb=pb: e.activation(
                    out=cx.vh[:, tt * 1024 + ob * 512:tt * 1024 + ob * 512 + 512],
                    in_=pb, func=AF.Copy))
            em.barrier()
        if b == 0:
            dbg(cx, "ckv", ckv, 4096)
            dbg(cx, "k_tm", cx.vh, 16384)
        for tt in range(TT):
            em.dma("sync", lambda e, tt=tt: e.dma_start(
                out=cx.aux[:, 0:1024], in_=d["rope_cos"][tt * 128:tt * 128 + 128, :]))
            em.dma("sync", lambda e, tt=tt: e.dma_start(
                out=cx.aux[:, 1024:2048], in_=d["rope_sin"][tt * 128:tt * 128 + 128, :]))
            em.barrier(cons=("vector",))
            for (buf, off) in ((cx.bigA, tt * 1024), (cx.vh, tt * 1024)):
                qv = buf[:, off:off + 1024].rearrange("p (h t s) -> p h t s", t=2, s=32)
                sv = cx.aux[:, 1024:2048].rearrange("p (h t s) -> p h t s", t=2, s=32)
                t1 = cx.rtmp[:, 0:1024]
                t2 = cx.rtmp[:, 1024:2048]
                t2v = t2.rearrange("p (h t s) -> p h t s", t=2, s=32)
                em.op("vector", lambda e, buf=buf, off=off, t1=t1: e.tensor_mul(
                    t1, buf[:, off:off + 1024], cx.aux[:, 0:1024]))
                em.op("vector", lambda e, qv=qv, sv=sv, t2v=t2v: e.tensor_mul(
                    t2v[:, :, 0, :], qv[:, :, 1, :], sv[:, :, 0, :]))
                em.op("vector", lambda e, qv=qv, sv=sv, t2v=t2v: e.tensor_mul(
                    t2v[:, :, 1, :], qv[:, :, 0, :], sv[:, :, 1, :]))
                em.barrier(cons=("vector",))
                em.op("vector", lambda e, buf=buf, off=off, t1=t1, t2=t2: e.tensor_add(
                    buf[:, off:off + 1024], t1, t2))
                em.barrier(cons=("vector",))
            em.barrier()
        if b == 0:
            dbg(cx, "qr_tm", cx.bigA, 16384)
            dbg(cx, "kr_tm", cx.vh, 16384)
        transpose_pass(cx, cx.bigA, lambda tt, ch: tt * 1024 + ch * 128,
                       cx.bigD, lambda tt, ch: ch * 2048 + tt * 128)
        transpose_pass(cx, cx.vh, lambda tt, ch: tt * 1024 + ch * 128,
                       cx.bigC, lambda tt, ch: ch * 2048 + tt * 128)
        em.op("vector", lambda e: e.memset(cx.vh[:, :], 1.0))
        em.barrier()
        for tt in range(TT):
            for ob in range(2):
                pb = PBUFS(cx)[ob]
                em.op("tensor", lambda e, tt=tt, ob=ob, pb=pb: e.matmul(
                    out=pb, lhsT=ckv[:, 2048 + tt * 128:2048 + tt * 128 + 128],
                    rhs=cx.wbf[:, 3072 + ob * 512:3072 + ob * 512 + 512],
                    start=True, stop=False, skip_group_check=True))
                em.op("tensor", lambda e, ob=ob, pb=pb: e.matmul(
                    out=pb, lhsT=cx.onesb[0:1, :],
                    rhs=cx.rows[0:1, 1024 + ob * 512:1024 + ob * 512 + 512],
                    start=False, stop=True, skip_group_check=True))
            em.barrier(cons=("scalar",))
            for ob in range(2):
                pb = PBUFS(cx)[ob]
                dst = cx.vh[:, tt * 1040 + ob * 520:tt * 1040 + (ob + 1) * 520]
                dv = dst.rearrange("p (h x) -> p h x", x=65)[:, :, 0:64]
                pv = pb.rearrange("p (h x) -> p h x", x=64)
                em.op("scalar", lambda e, dv=dv, pv=pv: e.activation(
                    out=dv, in_=pv, func=AF.Copy))
            em.barrier()
        if b == 0:
            dbg(cx, "vaug", cx.vh, 16640)
            dbg(cx, "qr_fm", cx.bigD, 16384)
            dbg(cx, "kr_fm", cx.bigC, 16384)
        attention(cx, kbuf=cx.bigC, with_bias=False)
        if b == 0:
            dbg(cx, "attn", cx.bigD, 16384)
        load_weight_bf16(cx, d["l0_wo"], cx.wbf, 0, 1024, 1024)
        load_bias_row(cx, d["l0_bo"], 2048, 1024)
        proj_postln(cx, b, xin, xout, brow_off=2048)
    em.barrier(cons=("sync",))


def attention(cx, kbuf, with_bias):
    em = cx.em
    for h in range(H):
        r0 = 64 * (h % 2)
        chq = (h // 2) * 2048
        if with_bias:
            em.dma("sync", lambda e, h=h: e.dma_start(out=cx.growb[0:1, :],
                                                      in_=cx.gall[h:h + 1, :]))
            if "tfl" not in SKIPS:
                for s, (side, delta) in enumerate(BAND_SLOTS):
                    si = 0 if side == "neg" else 1
                    base = (si * 16 + h) * 4095 + (1920 - delta)
                    win = RawAP(cx.tsrev_t, base, [[1, 128], [1, 512]])
                    em.dma("sync", lambda e, s=s, win=win: e.dma_start(
                        out=cx.aux[:, s * 512:s * 512 + 512], in_=win))
            em.barrier()
        tok_out_prev = None
        for qi in range(NQB):
            qoff = chq + qi * 512
            if with_bias:
                side = BAND_SIDES[qi]
                if any(t[0] == "rank1" for t in BAND_TILES[qi]):
                    dd = (cx.sbias[0:1, 24 + h:24 + h + 1] if side == "pos"
                          else cx.sbias[0:1, 40 + h:40 + h + 1])
                    em.op("vector", lambda e, qi=qi, dd=dd: e.tensor_scalar(
                        out=cx.rowt[0:1, :],
                        in0=cx.growb[0:1, qi * 512:qi * 512 + 512],
                        scalar1=dd, scalar2=None, op0=OP.mult))
                    em.barrier(cons=("tensor",))
                if any(t[0] == "band" for t in BAND_TILES[qi]):
                    em.op("tensor", lambda e, qi=qi: e.matmul(
                        out=cx.PB[:, :], lhsT=cx.onesb[0:1, :],
                        rhs=cx.growb[0:1, qi * 512:qi * 512 + 512],
                        start=True, stop=True, skip_group_check=True))
                    em.barrier(cons=("vector",))
                    em.op("vector", lambda e: e.tensor_copy(out=cx.gt[:, :],
                                                            in_=cx.PB[:, :]))
                    em.barrier(cons=("vector", "tensor"))
            toks_exp = {}
            toks_out = {}
            tEs = {}
            for kt in range(KT):
                half = kt % 2
                pS = cx.P0[:, half * 512:half * 512 + 512]
                t = BAND_TILES[qi][kt] if with_bias else ("skip",)
                if t[0] == "band" and "band" in SKIPS:
                    t = ("skip",)
                if t[0] == "rank1" and "rank1" in SKIPS:
                    t = ("skip",)
                if kt >= 2:
                    em.wait("tensor", toks_exp[kt - 2])
                if kt == 0 and tok_out_prev is not None:
                    em.wait("tensor", tok_out_prev)
                nojmm = "jmm" in SKIPS
                if t[0] == "band":
                    bb = cx.biasb[:, half * 512:half * 512 + 512]
                    em.barrier(cons=("vector",))
                    em.op("vector", lambda e, t=t, bb=bb: e.tensor_mul(
                        bb, cx.aux[:, t[1] * 512:t[1] * 512 + 512], cx.gt[:, :]))
                    tv = em.tok("vector")
                    em.wait("tensor", tv)
                em.op("tensor", lambda e, kt=kt, pS=pS, qoff=qoff, r0=r0, chq=chq,
                      last=(t[0] == "skip" or (t[0] == "band" and nojmm)),
                      kb=kbuf: e.matmul(
                    out=pS, lhsT=kb[r0:r0 + 64, chq + kt * 128:chq + kt * 128 + 128],
                    rhs=cx.bigD[r0:r0 + 64, qoff:qoff + 512],
                    start=True, stop=last, skip_group_check=True))
                if t[0] == "rank1":
                    em.op("tensor", lambda e, pS=pS: e.matmul(
                        out=pS, lhsT=cx.onesb[0:1, :], rhs=cx.rowt[0:1, :],
                        start=False, stop=True, skip_group_check=True))
                elif t[0] == "band" and not nojmm:
                    bb = cx.biasb[:, half * 512:half * 512 + 512]
                    em.op("tensor", lambda e, pS=pS, bb=bb: e.matmul(
                        out=pS, lhsT=cx.jrev[:, :], rhs=bb,
                        start=False, stop=True, skip_group_check=True))
                tEs[kt] = em.tok("tensor")
                em.wait("scalar", tEs[kt])
                if kt >= 2:
                    em.wait("scalar", toks_out[kt - 2])
                pt = cx.ptile[:, half * 512:half * 512 + 512]
                em.op("scalar", lambda e, pS=pS, pt=pt: e.activation(
                    out=pt, in_=pS, func=AF.Exp, scale=0.125))
                toks_exp[kt] = em.tok("scalar")
                em.wait("tensor", toks_exp[kt])
                em.op("tensor", lambda e, kt=kt, h=h, pt=pt: e.matmul(
                    out=cx.P2[0:65, :],
                    lhsT=cx.vh[:, kt * 1040 + h * 65:kt * 1040 + h * 65 + 65],
                    rhs=pt, start=(kt == 0), stop=(kt == KT - 1),
                    skip_group_check=True))
                toks_out[kt] = em.tok("tensor")
            em.wait("vector", toks_out[KT - 1])
            em.op("vector", lambda e: e.reciprocal(cx.fst[0:1, 1024:1536],
                                                   cx.P2[64:65, :]))
            em.barrier(cons=("scalar", "tensor"))
            em.op("scalar", lambda e: e.activation(out=cx.fst[0:64, 2048:2560],
                                                   in_=cx.P2[0:64, :], func=AF.Copy))
            em.op("tensor", lambda e: e.matmul(out=cx.P3[0:64, :],
                                               lhsT=cx.onesf[0:1, 0:64],
                                               rhs=cx.fst[0:1, 1024:1536],
                                               start=True, stop=True,
                                               skip_group_check=True))
            em.barrier(cons=("vector",))
            em.op("vector", lambda e, r0=r0, qoff=qoff: e.tensor_mul(
                cx.bigD[r0:r0 + 64, qoff:qoff + 512], cx.fst[0:64, 2048:2560],
                cx.P3[0:64, :]))
            tok_out_prev = em.tok("vector")
        em.barrier()


def proj_postln(cx, b, xin, xout, brow_off):
    em = cx.em
    for tt in range(TT):
        for ob in range(2):
            pb = PBUFS(cx)[ob]
            for ch in range(NCH):
                em.op("tensor", lambda e, tt=tt, ob=ob, ch=ch, pb=pb: e.matmul(
                    out=pb,
                    lhsT=cx.bigD[:, ch * 2048 + tt * 128:ch * 2048 + tt * 128 + 128],
                    rhs=cx.wbf[:, ch * 1024 + ob * 512:ch * 1024 + ob * 512 + 512],
                    start=(ch == 0), stop=False, skip_group_check=True))
            em.op("tensor", lambda e, ob=ob, pb=pb: e.matmul(
                out=pb, lhsT=cx.onesb[0:1, :],
                rhs=cx.rows[0:1, brow_off + ob * 512:brow_off + ob * 512 + 512],
                start=False, stop=True, skip_group_check=True))
        em.barrier(cons=("scalar",))
        for ob in range(2):
            pb = PBUFS(cx)[ob]
            em.op("scalar", lambda e, ob=ob, pb=pb: e.activation(
                out=f1(cx)[:, ob * 512:ob * 512 + 512], in_=pb, func=AF.Copy))
        em.barrier()
        if tt == 0 and b == 0:
            dbg(cx, "proj0", cx.fst, 2048)
        postln_tile(cx, b, tt, f1(cx), xin, xout)



def emit_l0_ffn(cx, d, xin, xout):
    em = cx.em
    load_const_setup(cx, d)
    load_weight_bf16(cx, d["f_bwT"], cx.wbf, 0, 1024, 1024)
    load_weight_bf16(cx, d["f_saT"], cx.ws, 0, 1024, 16)
    em.dma("sync", lambda e: e.dma_start(out=f0(cx)[0:16, :], in_=d["f_sbT"]))
    em.barrier(cons=("vector",))
    em.op("vector", lambda e: e.tensor_copy(out=cx.ws[0:16, 128:1152],
                                            in_=f0(cx)[0:16, :]))
    em.barrier()
    load_weight_bf16(cx, d["f_raT"], cx.ws, 1152, 1024, 96)
    em.dma("sync", lambda e: e.dma_start(out=f0(cx)[0:96, :], in_=d["f_rb"]))
    em.barrier(cons=("vector",))
    em.op("vector", lambda e: e.tensor_copy(out=cx.ws[0:96, 1920:2944],
                                            in_=f0(cx)[0:96, :]))
    em.barrier()
    load_weight_bf16(cx, d["f_rwT"], cx.ws, 2944, 1024, 6)
    em.dma("sync", lambda e: e.dma_start(out=cx.ws[0:6, 2992:3088], in_=d["f_esel"]))
    load_bias_row(cx, d["f_bb"], 0, 1024)
    load_bias_row(cx, d["f_rbias"], 2048, 6)
    em.barrier()

    for b in range(NB):
        ln_pass(cx, xin, b, cx.bigA, lambda tt: tt * 1024)
        transpose_pass(cx, cx.bigA, lambda tt, ch: tt * 1024 + ch * 128,
                       cx.bigC, lambda tt, ch: ch * 2048 + tt * 128)
        for tb in range(NQB):
            for ch in range(NCH):
                em.op("tensor", lambda e, tb=tb, ch=ch: e.matmul(
                    out=cx.PA[0:96, :],
                    lhsT=cx.ws[:, 1152 + ch * 96:1152 + ch * 96 + 96],
                    rhs=cx.bigC[:, ch * 2048 + tb * 512:ch * 2048 + tb * 512 + 512],
                    start=(ch == 0), stop=(ch == NCH - 1), skip_group_check=True))
            em.barrier(cons=("vector",))
            em.op("vector", lambda e, tb=tb: e.tensor_copy(
                out=cx.bigD[0:96, tb * 512:tb * 512 + 512], in_=cx.PA[0:96, :]))
            em.barrier()
        for tb in range(NQB):
            for ch in range(NCH):
                em.op("tensor", lambda e, tb=tb, ch=ch: e.matmul(
                    out=cx.PA[0:16, :], lhsT=cx.ws[:, ch * 16:ch * 16 + 16],
                    rhs=cx.bigC[:, ch * 2048 + tb * 512:ch * 2048 + tb * 512 + 512],
                    start=(ch == 0), stop=(ch == NCH - 1), skip_group_check=True))
            em.barrier(cons=("vector",))
            em.op("vector", lambda e, tb=tb: e.tensor_copy(
                out=cx.vh[0:16, tb * 512:tb * 512 + 512], in_=cx.PA[0:16, :]))
            em.barrier()
        for tt in range(TT):
            for ch in range(NCH):
                em.op("tensor", lambda e, tt=tt, ch=ch: e.matmul(
                    out=cx.PA[:, 0:6],
                    lhsT=cx.bigC[:, ch * 2048 + tt * 128:ch * 2048 + tt * 128 + 128],
                    rhs=cx.ws[:, 2944 + ch * 6:2944 + ch * 6 + 6],
                    start=(ch == 0), stop=False, skip_group_check=True))
            em.op("tensor", lambda e: e.matmul(
                out=cx.PA[:, 0:6], lhsT=cx.onesb[0:1, :], rhs=cx.rows[0:1, 2048:2054],
                start=False, stop=True, skip_group_check=True))
            em.barrier(cons=("vector",))
            g = f2(cx)[:, 0:6]
            logit = f2(cx)[:, 8:14]
            wv = f2(cx)[:, 16:22]
            tmp6 = f2(cx)[:, 24:30]
            m1 = cx.scol[:, 0:1]; m2 = cx.scol[:, 1:2]; ssum = cx.scol[:, 2:3]
            nm = cx.scol[:, 3:4]
            em.op("vector", lambda e: e.tensor_copy(out=logit, in_=cx.PA[:, 0:6]))
            em.barrier(cons=("vector",))
            em.op("vector", lambda e: e.reduce_max(m1, logit, axis=AX.X))
            em.barrier(cons=("vector",))
            em.op("vector", lambda e: e.tensor_scalar_mul(nm, m1, -1.0))
            em.barrier(cons=("scalar",))
            em.op("scalar", lambda e: e.activation(out=g, in_=logit, func=AF.Exp,
                                                   bias=nm))
            em.barrier(cons=("vector",))
            em.op("vector", lambda e: e.reduce_sum(ssum, g, axis=AX.X))
            em.barrier(cons=("vector",))
            em.op("vector", lambda e: e.reciprocal(m2, ssum))
            em.barrier(cons=("vector",))
            em.op("vector", lambda e: e.tensor_scalar_mul(g, g, m2))
            em.barrier(cons=("vector",))
            em.op("vector", lambda e: e.reduce_max(m1, g, axis=AX.X))
            em.barrier(cons=("vector",))
            em.op("vector", lambda e: e.tensor_scalar(out=cx.msk6[:, :], in0=g,
                                                      scalar1=m1, scalar2=None,
                                                      op0=OP.is_ge))
            em.barrier(cons=("vector",))
            em.op("vector", lambda e: e.select(tmp6, cx.msk6[:, :], cx.neg6[:, :], g))
            em.barrier(cons=("vector",))
            em.op("vector", lambda e: e.reduce_max(m2, tmp6, axis=AX.X))
            em.barrier(cons=("vector",))
            em.op("vector", lambda e: e.tensor_scalar(out=cx.msk6[:, :], in0=g,
                                                      scalar1=m2, scalar2=None,
                                                      op0=OP.is_ge))
            em.barrier(cons=("vector",))
            em.op("vector", lambda e: e.select(wv, cx.msk6[:, :], g, cx.zero6[:, :]))
            em.barrier(cons=("vector",))
            em.op("vector", lambda e: e.reduce_sum(ssum, wv, axis=AX.X))
            em.barrier(cons=("vector",))
            em.op("vector", lambda e: e.tensor_scalar_add(ssum, ssum, 1e-6))
            em.barrier(cons=("vector",))
            em.op("vector", lambda e: e.reciprocal(m2, ssum))
            em.barrier(cons=("vector",))
            em.op("vector", lambda e: e.memset(cx.rtmp[:, 0:128], 0.0))
            em.barrier(cons=("vector",))
            em.op("vector", lambda e: e.tensor_scalar(out=cx.rtmp[:, 0:6], in0=wv,
                                                      scalar1=m2, scalar2=None,
                                                      op0=OP.mult))
            em.barrier(cons=("tensor",))
            em.op("tensor", lambda e: e.transpose(out=cx.PT[:, :], in_=cx.rtmp[:, 0:128],
                                                  identity=cx.ident[:, :]))
            em.barrier(cons=("vector",))
            em.op("vector", lambda e: e.tensor_copy(out=cx.rtmp[:, 128:256],
                                                    in_=cx.PT[:, :]))
            em.barrier(cons=("tensor",))
            em.op("tensor", lambda e: e.matmul(out=cx.PB[0:96, 0:128],
                                               lhsT=cx.ws[0:6, 2992:3088],
                                               rhs=cx.rtmp[0:6, 128:256],
                                               start=True, stop=True,
                                               skip_group_check=True))
            em.barrier(cons=("vector",))
            em.op("vector", lambda e, tt=tt: e.tensor_mul(
                cx.rtmp[0:96, 256:384], cx.bigD[0:96, tt * 128:tt * 128 + 128],
                cx.PB[0:96, 0:128]))
            em.barrier(cons=("tensor",))
            for ob in range(2):
                pb = PBUFS(cx)[ob]
                for ch in range(NCH):
                    em.op("tensor", lambda e, tt=tt, ob=ob, ch=ch, pb=pb: e.matmul(
                        out=pb,
                        lhsT=cx.bigC[:, ch * 2048 + tt * 128:ch * 2048 + tt * 128 + 128],
                        rhs=cx.wbf[:, ch * 1024 + ob * 512:ch * 1024 + ob * 512 + 512],
                        start=(ch == 0), stop=False, skip_group_check=True))
                em.op("tensor", lambda e, tt=tt, ob=ob, pb=pb: e.matmul(
                    out=pb, lhsT=cx.vh[0:16, tt * 128:tt * 128 + 128],
                    rhs=cx.ws[0:16, 128 + ob * 512:128 + ob * 512 + 512],
                    start=False, stop=False, skip_group_check=True))
                em.op("tensor", lambda e, ob=ob, pb=pb: e.matmul(
                    out=pb, lhsT=cx.rtmp[0:96, 256:384],
                    rhs=cx.ws[0:96, 1920 + ob * 512:1920 + ob * 512 + 512],
                    start=False, stop=False, skip_group_check=True))
                em.op("tensor", lambda e, ob=ob, pb=pb: e.matmul(
                    out=pb, lhsT=cx.onesb[0:1, :],
                    rhs=cx.rows[0:1, ob * 512:ob * 512 + 512],
                    start=False, stop=True, skip_group_check=True))
            em.barrier(cons=("scalar",))
            for ob in range(2):
                pb = PBUFS(cx)[ob]
                em.op("scalar", lambda e, ob=ob, pb=pb: e.activation(
                    out=f1(cx)[:, ob * 512:ob * 512 + 512], in_=pb, func=AF.Copy))
            em.barrier()
            postln_tile(cx, b, tt, f1(cx), xin, xout)
    em.barrier(cons=("sync",))



def emit_l1_attn(cx, d, xin, xout):
    em = cx.em
    load_const_setup(cx, d)
    em.dma("sync", lambda e: e.dma_start(out=cx.sbias[0:16, 73:74], in_=d["a_cneg_col"]))
    em.dma("sync", lambda e: e.dma_start(out=cx.sbias[0:16, 74:75], in_=d["a_cpos_col"]))
    em.dma("sync", lambda e: e.dma_start(out=cx.sbias[0:8, 75:76], in_=d["a_grub"]))
    em.dma("sync", lambda e: e.dma_start(out=cx.sbias[8:16, 75:76], in_=d["a_grub"]))
    em.dma("sync", lambda e: e.dma_start(out=cx.sbias[0:1, 56:72], in_=d["a_gc"]))
    em.barrier()
    if "ts" not in SKIPS:
        build_ts_tables(cx, d)
    em.dma("sync", lambda e: e.dma_start(out=f0(cx)[:, 0:16], in_=d["a_gruwbd"]))
    em.barrier(cons=("vector",))
    em.op("vector", lambda e: e.tensor_copy(out=cx.ws[:, 0:16], in_=f0(cx)[:, 0:16]))
    em.barrier()

    for b in range(NB):
        ln_pass(cx, xin, b, cx.bigA, lambda tt: tt * 1024)
        transpose_pass(cx, cx.bigA, lambda tt, ch: tt * 1024 + ch * 128,
                       cx.bigC, lambda tt, ch: ch * 2048 + tt * 128)
        if "gates" not in SKIPS:
            gates(cx, d)
        else:
            em.op("vector", lambda e: e.memset(cx.gall[:, :], 1.0))
            em.barrier()
        load_weight_bf16(cx, d["a_wq"], cx.wbf, 0, 1024, 1024)
        qkv_fm(cx, d, "a_bq", cx.bigD)
        load_weight_bf16(cx, d["a_wk"], cx.wbf, 0, 1024, 1024)
        qkv_fm(cx, d, "a_bk", cx.bigA)
        load_weight_bf16(cx, d["a_wv"], cx.wbf, 0, 1024, 1024)
        load_bias_row(cx, d["a_bv"], 1024, 1024)
        em.op("vector", lambda e: e.memset(cx.vh[:, :], 1.0))
        em.barrier()
        for tt in range(TT):
            for ob in range(2):
                pb = PBUFS(cx)[ob]
                for ic in range(NCH):
                    em.op("tensor", lambda e, tt=tt, ob=ob, ic=ic, pb=pb: e.matmul(
                        out=pb,
                        lhsT=cx.bigC[:, ic * 2048 + tt * 128:ic * 2048 + tt * 128 + 128],
                        rhs=cx.wbf[:, ic * 1024 + ob * 512:ic * 1024 + ob * 512 + 512],
                        start=(ic == 0), stop=False, skip_group_check=True))
                em.op("tensor", lambda e, ob=ob, pb=pb: e.matmul(
                    out=pb, lhsT=cx.onesb[0:1, :],
                    rhs=cx.rows[0:1, 1024 + ob * 512:1024 + ob * 512 + 512],
                    start=False, stop=True, skip_group_check=True))
            em.barrier(cons=("scalar",))
            for ob in range(2):
                pb = PBUFS(cx)[ob]
                dst = cx.vh[:, tt * 1040 + ob * 520:tt * 1040 + (ob + 1) * 520]
                dv = dst.rearrange("p (h x) -> p h x", x=65)[:, :, 0:64]
                pv = pb.rearrange("p (h x) -> p h x", x=64)
                em.op("scalar", lambda e, dv=dv, pv=pv: e.activation(
                    out=dv, in_=pv, func=AF.Copy))
            em.barrier()
        attention(cx, kbuf=cx.bigA, with_bias=("bias" not in SKIPS))
        load_weight_bf16(cx, d["a_wo"], cx.wbf, 0, 1024, 1024)
        load_bias_row(cx, d["a_bo"], 2048, 1024)
        proj_postln(cx, b, xin, xout, brow_off=2048)
    em.barrier(cons=("sync",))


def build_ts_tables(cx, d):
    em = cx.em
    tr = cx.fst[0:16, 0:4095]
    em.dma("sync", lambda e: e.dma_start(out=tr, in_=d["a_trev"]))
    em.barrier()
    for si, col in ((0, 73), (1, 74)):
        em.op("vector", lambda e, col=col: e.tensor_scalar(
            out=cx.wbf[0:16, 0:4095], in0=tr, scalar1=cx.sbias[0:16, col:col + 1],
            scalar2=None, op0=OP.subtract))
        em.barrier(cons=("sync",))
        em.dma("sync", lambda e, si=si: e.dma_start(
            out=cx.tsrev_dram[si, :, :], in_=cx.wbf[0:16, 0:4095]))
        em.barrier()
    em.dma("sync", lambda e: e.dma_start(out=cx.fst[0:1, 0:16], in_=d["a_cneg_row"]))
    em.dma("sync", lambda e: e.dma_start(out=cx.fst[0:1, 16:32], in_=d["a_cpos_row"]))
    em.barrier()
    em.op("vector", lambda e: e.tensor_sub(cx.sbias[0:1, 24:40], cx.fst[0:1, 0:16],
                                           cx.fst[0:1, 16:32]))
    em.op("vector", lambda e: e.tensor_sub(cx.sbias[0:1, 40:56], cx.fst[0:1, 16:32],
                                           cx.fst[0:1, 0:16]))
    em.barrier()


def gates(cx, d):
    em = cx.em
    for ch in range(NCH):
        for tb in range(NQB):
            em.op("tensor", lambda e, ch=ch, tb=tb: e.matmul(
                out=cx.PA[0:16, :], lhsT=cx.ws[:, 0:16],
                rhs=cx.bigC[:, ch * 2048 + tb * 512:ch * 2048 + tb * 512 + 512],
                start=True, stop=True, skip_group_check=True))
            em.barrier(cons=("vector",))
            em.op("vector", lambda e: e.tensor_scalar_add(cx.gsb[:, :], cx.PA[0:16, :],
                                                          cx.sbias[0:16, 75:76]))
            em.barrier(cons=("tensor",))
            em.op("tensor", lambda e: e.matmul(out=cx.PB[0:4, :], lhsT=cx.pairc[:, :],
                                               rhs=cx.gsb[:, :], start=True, stop=True,
                                               skip_group_check=True))
            em.barrier(cons=("scalar",))
            em.op("scalar", lambda e: e.activation(out=cx.fst[0:4, 0:512],
                                                   in_=cx.PB[0:4, :], func=AF.Sigmoid))
            em.barrier(cons=("sync",))
            for j in range(4):
                em.dma("sync", lambda e, j=j: e.dma_start(
                    out=cx.fst[0:1, 1024 + j * 512:1024 + (j + 1) * 512],
                    in_=cx.fst[j:j + 1, 0:512]))
            em.barrier()
            for hh in range(2):
                h = 2 * ch + hh
                ga = cx.fst[0:1, 1024 + (2 * hh) * 512:1024 + (2 * hh) * 512 + 512]
                gb = cx.fst[0:1, 1024 + (2 * hh + 1) * 512:1024 + (2 * hh + 1) * 512 + 512]
                t2 = cx.fst[0:1, 3072:3584]
                t3 = cx.fst[0:1, 3584:4096]
                em.op("vector", lambda e, gb=gb, h=h, t2=t2: e.tensor_scalar(
                    out=t2, in0=gb, scalar1=cx.sbias[0:1, 56 + h:56 + h + 1],
                    scalar2=-1.0, op0=OP.mult, op1=OP.add))
                em.barrier(cons=("vector",))
                em.op("vector", lambda e, ga=ga, t2=t2, t3=t3: e.tensor_mul(t3, ga, t2))
                em.barrier(cons=("vector",))
                em.op("vector", lambda e, t3=t3: e.tensor_scalar(
                    out=cx.growb[0:1, 0:512], in0=t3, scalar1=2.0, scalar2=8.0,
                    op0=OP.add, op1=OP.mult))
                em.barrier(cons=("sync",))
                em.dma("sync", lambda e, h=h, tb=tb: e.dma_start(
                    out=cx.gall[h:h + 1, tb * 512:tb * 512 + 512],
                    in_=cx.growb[0:1, 0:512]))
                em.barrier()


def qkv_fm(cx, d, bias_key, dst):
    em = cx.em
    for oc in range(NCH):
        em.dma("sync", lambda e, oc=oc: e.dma_start(
            out=cx.sbias[:, 3:4], in_=d[bias_key][oc * 128:oc * 128 + 128, :]))
        em.barrier(cons=("tensor",))
        for tb in range(NQB):
            pb = PBUFS(cx)[tb]
            for ic in range(NCH):
                em.op("tensor", lambda e, oc=oc, tb=tb, ic=ic, pb=pb: e.matmul(
                    out=pb,
                    lhsT=cx.wbf[:, ic * 1024 + oc * 128:ic * 1024 + oc * 128 + 128],
                    rhs=cx.bigC[:, ic * 2048 + tb * 512:ic * 2048 + tb * 512 + 512],
                    start=(ic == 0), stop=(ic == NCH - 1), skip_group_check=True))
        em.barrier(cons=("vector",))
        for tb in range(NQB):
            pb = PBUFS(cx)[tb]
            em.op("vector", lambda e, oc=oc, tb=tb, pb=pb: e.tensor_scalar_add(
                dst[:, oc * 2048 + tb * 512:oc * 2048 + tb * 512 + 512],
                pb, cx.sbias[:, 3:4]))
        em.barrier()



def emit_l1_ffn(cx, d, xin, xout):
    em = cx.em
    load_const_setup(cx, d)
    for i in range(NCH):
        em.dma("sync", lambda e, i=i: e.dma_start(
            out=cx.sbias[:, 8 + i:9 + i], in_=d["r_mur"][i * 128:i * 128 + 128, :]))
        em.dma("sync", lambda e, i=i: e.dma_start(
            out=cx.sbias[:, 16 + i:17 + i], in_=d["r_muk"][i * 128:i * 128 + 128, :]))
    em.barrier()
    for b in range(NB):
        ln_pass(cx, xin, b, cx.bigA, lambda tt: tt * 1024)
        transpose_pass(cx, cx.bigA, lambda tt, ch: tt * 1024 + ch * 128,
                       cx.bigC, lambda tt, ch: ch * 2048 + tt * 128)
        mix(cx, cx.bigD, mu_off=8)
        load_weight_bf16(cx, d["r_wrT"], cx.wbf, 0, 1024, 1024)
        fm_matmul_act(cx, cx.bigD, cx.bigA, AF.Sigmoid)
        mix(cx, cx.bigD, mu_off=16)
        load_weight_bf16(cx, d["r_wkT"], cx.wbf, 0, 1024, 1024)
        fm_matmul_act(cx, cx.bigD, cx.vh, AF.Relu)
        for ch in range(NCH):
            em.op("scalar", lambda e, ch=ch: e.activation(
                out=cx.bigD[:, ch * 2048:ch * 2048 + 2048],
                in_=cx.vh[:, ch * 2048:ch * 2048 + 2048], func=AF.Square))
        em.barrier()
        transpose_pass(cx, cx.bigA, lambda tt, ch: ch * 2048 + tt * 128,
                       cx.bigC, lambda tt, ch: tt * 1024 + ch * 128)
        load_weight_bf16(cx, d["r_wvT"], cx.wbf, 0, 1024, 1024)
        for tt in range(TT):
            for ob in range(2):
                pb = PBUFS(cx)[ob]
                for ic in range(NCH):
                    em.op("tensor", lambda e, tt=tt, ob=ob, ic=ic, pb=pb: e.matmul(
                        out=pb,
                        lhsT=cx.bigD[:, ic * 2048 + tt * 128:ic * 2048 + tt * 128 + 128],
                        rhs=cx.wbf[:, ic * 1024 + ob * 512:ic * 1024 + ob * 512 + 512],
                        start=(ic == 0), stop=(ic == NCH - 1), skip_group_check=True))
            em.barrier(cons=("vector",))
            for ob in range(2):
                pb = PBUFS(cx)[ob]
                em.op("vector", lambda e, tt=tt, ob=ob, pb=pb: e.tensor_mul(
                    f1(cx)[:, ob * 512:ob * 512 + 512], pb,
                    cx.bigC[:, tt * 1024 + ob * 512:tt * 1024 + ob * 512 + 512]))
            em.barrier()
            postln_tile(cx, b, tt, f1(cx), xin, xout)
    em.barrier(cons=("sync",))


def mix(cx, dst, mu_off):
    em = cx.em
    for ch in range(NCH):
        o = ch * 2048
        dd = cx.rtmp[:, 0:2048]
        em.op("vector", lambda e, o=o, dd=dd: e.tensor_sub(
            dd[:, 1:2048], cx.bigC[:, o + 1:o + 2048], cx.bigC[:, o:o + 2047]))
        em.barrier(cons=("vector",))
        mu = cx.sbias[:, mu_off + ch:mu_off + ch + 1]
        em.op("vector", lambda e, o=o, dd=dd, mu=mu: e.scalar_tensor_tensor(
            out=dst[:, o + 1:o + 2048], in0=dd[:, 1:2048], scalar=mu,
            in1=cx.bigC[:, o:o + 2047], op0=OP.mult, op1=OP.add))
        em.op("vector", lambda e, o=o, mu=mu: e.tensor_scalar_mul(
            dst[:, o:o + 1], cx.bigC[:, o:o + 1], mu))
        em.barrier(cons=("vector",))
    em.barrier()


def fm_matmul_act(cx, src, dst, func):
    em = cx.em
    for oc in range(NCH):
        for tb in range(NQB):
            pb = PBUFS(cx)[tb]
            for ic in range(NCH):
                em.op("tensor", lambda e, oc=oc, tb=tb, ic=ic, pb=pb: e.matmul(
                    out=pb,
                    lhsT=cx.wbf[:, ic * 1024 + oc * 128:ic * 1024 + oc * 128 + 128],
                    rhs=src[:, ic * 2048 + tb * 512:ic * 2048 + tb * 512 + 512],
                    start=(ic == 0), stop=(ic == NCH - 1), skip_group_check=True))
        em.barrier(cons=("scalar",))
        for tb in range(NQB):
            pb = PBUFS(cx)[tb]
            em.op("scalar", lambda e, oc=oc, tb=tb, pb=pb, func=func: e.activation(
                out=dst[:, oc * 2048 + tb * 512:oc * 2048 + tb * 512 + 512],
                in_=pb, func=func))
        em.barrier()


import os
DBG_TAG = os.environ.get("KDBG", "")
SKIPS = set(os.environ.get("KSKIP", "").split(","))


def dbg(cx, tag, buf, ncols):
    if DBG_TAG != tag:
        return
    em = cx.em
    em.barrier()
    for part in range(0, ncols, 2048):
        w = min(2048, ncols - part)
        em.dma("gpsimd", lambda e, part=part, w=w: e.dma_start(
            out=cx.dbg_dram[:, part:part + w], in_=buf[:, part:part + w]))
    em.barrier()



STAGES = ("l0a", "l0f", "l1a", "l1f")
EMITTERS = {"l0a": emit_l0_attn, "l0f": emit_l0_ffn,
            "l1a": emit_l1_attn, "l1f": emit_l1_ffn}
_NC_CACHE = {}


def build_stage_nc(stage, hc_shapes):
    key = stage
    if key in _NC_CACHE:
        return _NC_CACHE[key]
    nc = bass.Bass("TRN2", target_bir_lowering=False)
    xin = nc.dram_tensor("xin", [NB * T, D], F32, kind="ExternalInput").ap()
    xout = nc.dram_tensor("xout", [NB * T, D], F32, kind="ExternalOutput").ap()
    d = {}
    for k in STAGE_KEYS[stage]:
        sh, dt = hc_shapes[k]
        d[k] = nc.dram_tensor(k, list(sh), dt, kind="ExternalInput").ap()
    cx = make_ctx(nc)
    cx.dbg_dram = nc.dram_tensor("dbgout", [128, 16640], BF16,
                                 kind="ExternalOutput").ap()
    if stage == "l1a":
        th = nc.dram_tensor("tsrev", [2, 16, 4095], BF16, kind="Internal")
        cx.tsrev_dram = th.ap()
        cx.tsrev_t = cx.tsrev_dram.tensor
    EMITTERS[stage](cx, d, xin, xout)
    with nc.Block() as blk:
        cx.em.replay(blk)
    cx.ctx.close()
    _NC_CACHE[key] = nc
    return nc


ALL_KEYS = sorted({k for ks in STAGE_KEYS.values() for k in ks})


def build_fused_nc(hc_shapes):
    if "fused" in _NC_CACHE:
        return _NC_CACHE["fused"]
    nc = bass.Bass("TRN2", target_bir_lowering=False)
    xin = nc.dram_tensor("xin", [NB * T, D], F32, kind="ExternalInput").ap()
    xout = nc.dram_tensor("xout", [NB * T, D], F32, kind="ExternalOutput").ap()
    xm = [nc.dram_tensor(f"xmid{i}", [NB * T, D], F32, kind="Internal").ap()
          for i in range(3)]
    d = {}
    for k in ALL_KEYS:
        sh, dt = hc_shapes[k]
        d[k] = nc.dram_tensor(k, list(sh), dt, kind="ExternalInput").ap()
    cx = make_ctx(nc)
    cx.dbg_dram = nc.dram_tensor("dbgout", [128, 16640], BF16,
                                 kind="ExternalOutput").ap()
    th = nc.dram_tensor("tsrev", [2, 16, 4095], BF16, kind="Internal")
    cx.tsrev_dram = th.ap()
    cx.tsrev_t = cx.tsrev_dram.tensor
    emit_l0_attn(cx, d, xin, xm[0])
    cx.em.new_epoch(cx.sems[1])
    emit_l0_ffn(cx, d, xm[0], xm[1])
    cx.em.new_epoch(cx.sems[2])
    emit_l1_attn(cx, d, xm[1], xm[2])
    cx.em.new_epoch(cx.sems[3])
    emit_l1_ffn(cx, d, xm[2], xout)
    with nc.Block() as blk:
        cx.em.replay(blk)
    cx.ctx.close()
    _NC_CACHE["fused"] = nc
    return nc


def kernel(**inputs):
    hc = host_prep(inputs)
    x = np.ascontiguousarray(np.asarray(inputs["L0_attn_x"], np.float32)
                             .reshape(16 * T, D))
    shards = [x[c * NB * T:(c + 1) * NB * T, :] for c in range(NCORE)]
    hc_shapes = {k: (v.shape, BF16 if v.dtype == np.dtype(BF) else F32)
                 for k, v in hc.items()}
    if os.environ.get("KUNFUSED"):
        cur = shards
        for stage in STAGES:
            nc = build_stage_nc(stage, hc_shapes)
            in_maps = []
            for c in range(NCORE):
                m = {"xin": cur[c]}
                for k in STAGE_KEYS[stage]:
                    m[k] = hc[k]
                in_maps.append(m)
            res = run_bass_kernel_spmd(nc, in_maps, core_ids=list(range(NCORE)))
            cur = [np.asarray(r["xout"], np.float32) for r in res.results]
        return np.concatenate(cur, axis=0).reshape(16, T, D)
    nc = build_fused_nc(hc_shapes)
    in_maps = []
    for c in range(NCORE):
        m = {"xin": shards[c]}
        for k in ALL_KEYS:
            m[k] = hc[k]
        in_maps.append(m)
    res = run_bass_kernel_spmd(nc, in_maps, core_ids=list(range(NCORE)))
    cur = [np.asarray(r["xout"], np.float32) for r in res.results]
    return np.concatenate(cur, axis=0).reshape(16, T, D)


# version 06
# speedup vs baseline: 1.0523x; 1.0472x over previous
import numpy as np
import ml_dtypes
from contextlib import ExitStack
import concourse.bass as bass
import concourse.mybir as mybir
from concourse.bass_utils import run_bass_kernel_spmd
from bass_rust import AP as RawAP

F32 = mybir.dt.float32
BF16 = mybir.dt.bfloat16
U32 = mybir.dt.uint32
AF = mybir.ActivationFunctionType
OP = mybir.AluOpType
AX = mybir.AxisListType
BF = ml_dtypes.bfloat16

NCORE = 8
NB = 2
T = 2048
D = 1024
H, DH = 16, 64
TT = T // 128
NCH = D // 128
QB = 512
NQB = T // QB
KT = T // 128
NBUCK, MAXD = 32, 128
ENGS = ("sync", "tensor", "vector", "scalar", "gpsimd")
DMAQS = ("sync",)



class Em:
    def __init__(self, sem):
        self.sem = sem
        self.items = {e: [] for e in ENGS}
        self.val = 0
        self.dirty = {}

    def op(self, eng, fn):
        self.items[eng].append(["op", fn])
        self.dirty[eng] = True

    def dma(self, eng, fn):
        sem = self.sem

        def wrapped(e, fn=fn):
            return fn(e).then_inc(sem, 16)
        self.items[eng].append(["op", wrapped])
        self.val += 16
        return self.val

    def tok(self, eng):
        sem = self.sem
        item = self.items[eng][-1]
        assert item[0] == "op"
        fn = item[1]

        def wrapped(e, fn=fn):
            return fn(e).then_inc(sem, 1)
        item[1] = wrapped
        self.dirty.pop(eng, None)
        self.val += 1
        return self.val

    def wait(self, eng, v):
        self.items[eng].append(["wait", self.sem, v])

    def barrier(self, cons=ENGS):
        for eng in list(self.dirty):
            self.tok(eng)
        for eng in cons:
            self.items[eng].append(["wait", self.sem, self.val])

    def new_epoch(self, sem):
        self.barrier()
        self.sem = sem
        self.val = 0

    def replay(self, blk):
        for eng in ENGS:
            if not self.items[eng]:
                continue

            def body(e, eng=eng):
                last = {}
                for it in self.items[eng]:
                    if it[0] == "op":
                        it[1](e)
                    else:
                        sem, v = it[1], it[2]
                        k = id(sem)
                        if last.get(k, -1) < v:
                            e.wait_ge(sem, v)
                            last[k] = v
            getattr(blk, eng)(body)



def _rope_tables():
    inv = 1.0 / (10000.0 ** (np.arange(0, DH, 2, dtype=np.float64) / DH))
    fr = np.arange(T, dtype=np.float64)[:, None] * inv[None, :]
    emb = np.concatenate([fr, fr], axis=-1)
    cos = np.cos(emb)
    sin = np.sin(emb)
    cosr = np.tile(cos, (1, H)).astype(BF)
    sgn = np.concatenate([-np.ones((T, DH // 2)), np.ones((T, DH // 2))], axis=-1)
    sinr = np.tile(sin * sgn, (1, H)).astype(BF)
    return cosr, sinr


def _rel_buckets_1d():
    rel = np.arange(-(T - 1), T, dtype=np.int64)
    nb = NBUCK // 2
    buckets = (rel > 0).astype(np.int64) * nb
    rp = np.abs(rel)
    max_exact = nb // 2
    is_small = rp < max_exact
    large = max_exact + (np.log(np.maximum(rp, 1) / max_exact)
                         / np.log(MAXD / max_exact) * (nb - max_exact)).astype(np.int64)
    large = np.minimum(large, nb - 1)
    return buckets + np.where(is_small, rp, large)


_BUCK = _rel_buckets_1d()


def _band_plan():
    mid = T - 1
    r_pos = mid + 1
    for r in range(1, T):
        if (_BUCK[mid + r:] == _BUCK[-1]).all():
            r_pos = r
            break
    r_neg = mid + 1
    for r in range(1, T):
        if (_BUCK[:mid - r + 1] == _BUCK[0]).all():
            r_neg = r
            break
    sides, tiles, slots, slotmap = [], [], [], {}
    for qi in range(NQB):
        i0 = qi * QB
        cls = []
        npos = nneg = 0
        for ki in range(KT):
            j0 = ki * 128
            lo, hi = j0 - i0 - (QB - 1), j0 + 127 - i0
            if hi <= -r_neg:
                cls.append("far_neg"); nneg += 1
            elif lo >= r_pos:
                cls.append("far_pos"); npos += 1
            else:
                cls.append("band")
        side = "pos" if npos >= nneg else "neg"
        sides.append(side)
        row = []
        for ki in range(KT):
            c = cls[ki]
            if (c == "far_pos" and side == "pos") or (c == "far_neg" and side == "neg"):
                row.append(("skip",))
            elif c in ("far_pos", "far_neg"):
                row.append(("rank1",))
            else:
                key = (side, ki * 128 - i0)
                if key not in slotmap:
                    slotmap[key] = len(slots)
                    slots.append(key)
                row.append(("band", slotmap[key]))
        tiles.append(row)
    return sides, tiles, slots


BAND_SIDES, BAND_TILES, BAND_SLOTS = _band_plan()
NSLOT = len(BAND_SLOTS)


def host_prep(inp):
    f = np.float32
    hc = {}
    hc["l0_wdk"] = np.ascontiguousarray(inp["L0_attn_w_dk"], f)
    hc["l0_wdv"] = np.ascontiguousarray(inp["L0_attn_w_dv"], f)
    hc["l0_wuk"] = np.ascontiguousarray(inp["L0_attn_w_uk"], f)
    hc["l0_wuv"] = np.ascontiguousarray(inp["L0_attn_w_uv"], f)
    hc["l0_wo"] = np.ascontiguousarray(inp["L0_attn_w_o"], f)
    hc["l0_bdk"] = np.asarray(inp["L0_attn_b_dk"], f).reshape(128, 1)
    hc["l0_bdv"] = np.asarray(inp["L0_attn_b_dv"], f).reshape(128, 1)
    hc["l0_buk"] = np.asarray(inp["L0_attn_b_uk"], f).reshape(1, 1024)
    hc["l0_buv"] = np.asarray(inp["L0_attn_b_uv"], f).reshape(1, 1024)
    hc["l0_bo"] = np.asarray(inp["L0_attn_b_o"], f).reshape(1, 1024)
    cosr, sinr = _rope_tables()
    hc["rope_cos"] = cosr
    hc["rope_sin"] = sinr

    hc["f_bwT"] = np.ascontiguousarray(np.asarray(inp["L0_ffn_base_W"], f).T)
    hc["f_bb"] = np.asarray(inp["L0_ffn_base_b"], f).reshape(1, 1024)
    hc["f_saT"] = np.ascontiguousarray(np.asarray(inp["L0_ffn_shared_A"], f).T)
    hc["f_sbT"] = np.ascontiguousarray(np.asarray(inp["L0_ffn_shared_B"], f).T)
    ra = np.asarray(inp["L0_ffn_routed_A"], f)
    hc["f_raT"] = np.ascontiguousarray(ra.reshape(96, 1024).T)
    rb = np.asarray(inp["L0_ffn_routed_B"], f)
    hc["f_rb"] = np.ascontiguousarray(rb.transpose(0, 2, 1).reshape(96, 1024))
    hc["f_rwT"] = np.ascontiguousarray(np.asarray(inp["L0_ffn_router_W"], f).T)
    hc["f_rbias"] = np.asarray(inp["L0_ffn_router_b"], f).reshape(1, 6)
    esel = np.zeros((6, 96), dtype=BF)
    for e in range(6):
        esel[e, e * 16:(e + 1) * 16] = 1
    hc["f_esel"] = esel
    hc["ident"] = np.eye(128).astype(BF)
    hc["jrev"] = np.eye(128)[::-1].copy().astype(BF)

    hc["a_wq"] = np.ascontiguousarray(inp["L1_attn_Wq"], f)
    hc["a_wk"] = np.ascontiguousarray(inp["L1_attn_Wk"], f)
    hc["a_wv"] = np.ascontiguousarray(inp["L1_attn_Wv"], f)
    hc["a_wo"] = np.ascontiguousarray(inp["L1_attn_Wo"], f)
    hc["a_bq"] = np.asarray(inp["L1_attn_bq"], f).reshape(1024, 1)
    hc["a_bk"] = np.asarray(inp["L1_attn_bk"], f).reshape(1024, 1)
    hc["a_bv"] = np.asarray(inp["L1_attn_bv"], f).reshape(1, 1024)
    hc["a_bo"] = np.asarray(inp["L1_attn_bo"], f).reshape(1, 1024)
    re_ = np.asarray(inp["L1_attn_rel_embed"], f)
    ttab = re_[_BUCK, :].T
    hc["a_trev"] = np.ascontiguousarray(ttab[:, ::-1])
    hc["a_cneg_col"] = np.ascontiguousarray(ttab[:, 0].reshape(16, 1))
    hc["a_cpos_col"] = np.ascontiguousarray(ttab[:, -1].reshape(16, 1))
    hc["a_cneg_row"] = np.ascontiguousarray(ttab[:, 0].reshape(1, 16))
    hc["a_cpos_row"] = np.ascontiguousarray(ttab[:, -1].reshape(1, 16))
    gw = np.asarray(inp["L1_attn_gru_w"], f)
    bd = np.zeros((128, 16), dtype=f)
    bd[0:64, 0:8] = gw
    bd[64:128, 8:16] = gw
    hc["a_gruwbd"] = bd
    hc["a_grub"] = np.asarray(inp["L1_attn_gru_b"], f).reshape(8, 1)
    hc["a_gc"] = np.asarray(inp["L1_attn_gru_const"], f).reshape(1, 16)
    pc = np.zeros((16, 4), dtype=BF)
    pc[0:4, 0] = 1; pc[4:8, 1] = 1; pc[8:12, 2] = 1; pc[12:16, 3] = 1
    hc["a_pairc"] = pc

    hc["r_wrT"] = np.ascontiguousarray(np.asarray(inp["L1_ffn_W_r"], f).T)
    hc["r_wkT"] = np.ascontiguousarray(np.asarray(inp["L1_ffn_W_k"], f).T)
    hc["r_wvT"] = np.ascontiguousarray(np.asarray(inp["L1_ffn_W_v"], f).T)
    hc["r_mur"] = np.asarray(inp["L1_ffn_mu_r"], f).reshape(1024, 1)
    hc["r_muk"] = np.asarray(inp["L1_ffn_mu_k"], f).reshape(1024, 1)
    return hc


STAGE_KEYS = {
    "l0a": ["l0_wdk", "l0_wdv", "l0_wuk", "l0_wuv", "l0_wo", "l0_bdk", "l0_bdv",
            "l0_buk", "l0_buv", "l0_bo", "rope_cos", "rope_sin",
            "jrev", "ident", "a_pairc"],
    "l0f": ["f_bwT", "f_bb", "f_saT", "f_sbT", "f_raT", "f_rb", "f_rwT", "f_rbias",
            "f_esel", "jrev", "ident", "a_pairc"],
    "l1a": ["a_wq", "a_wk", "a_wv", "a_wo", "a_bq", "a_bk", "a_bv", "a_bo",
            "a_trev", "a_cneg_col", "a_cpos_col", "a_cneg_row", "a_cpos_row",
            "a_gruwbd", "a_grub", "a_gc", "a_pairc", "jrev", "ident"],
    "l1f": ["r_wrT", "r_wkT", "r_wvT", "r_mur", "r_muk", "jrev", "ident", "a_pairc"],
}



class Ctx:
    pass


def make_ctx(nc):
    cx = Ctx()
    cx.nc = nc
    cx.ctx = ExitStack()
    sb = lambda n, sh, dt: cx.ctx.enter_context(nc.sbuf_tensor(n, sh, dt))
    ps = lambda n, sh, dt: cx.ctx.enter_context(nc.psum_tensor(n, sh, dt))
    cx.bigA = sb("bigA", [128, 16384], BF16)
    cx.bigC = sb("bigC", [128, 16384], BF16)
    cx.bigD = sb("bigD", [128, 16384], BF16)
    cx.vh = sb("vh", [128, 16640], BF16)
    cx.fst = sb("fst", [128, 4096], F32)
    cx.wbf = sb("wbf", [128, 8192], BF16)
    cx.ws = sb("ws", [128, 3328], BF16)
    cx.rtmp = sb("rtmp", [128, 2048], BF16)
    cx.aux = sb("aux", [128, max(NSLOT, 4) * 512], BF16)
    cx.ptile = sb("ptile", [128, 1024], BF16)
    cx.gt = sb("gt", [128, 512], BF16)
    cx.biasb = sb("biasb", [128, 1024], BF16)
    cx.rowt = sb("rowt", [1, 512], BF16)
    cx.rows = sb("rows", [1, 3072], BF16)
    cx.gall = sb("gall", [16, 2048], BF16)
    cx.growb = sb("growb", [1, 2048], BF16)
    cx.scol = sb("scol", [128, 12], F32)
    cx.sbias = sb("sbias", [128, 96], F32)
    cx.onesf = sb("onesf", [1, 128], F32)
    cx.onesb = sb("onesb", [1, 128], BF16)
    cx.jrev = sb("c_jrev", [128, 128], BF16)
    cx.ident = sb("c_ident", [128, 128], BF16)
    cx.pairc = sb("c_pairc", [16, 4], BF16)
    cx.gsb = sb("gsb", [16, 512], BF16)
    cx.msk6 = cx.ctx.enter_context(nc.sbuf_tensor("msk6", [128, 6], U32))
    cx.neg6 = sb("neg6", [128, 6], F32)
    cx.zero6 = sb("zero6", [128, 6], F32)
    cx.P0 = ps("P0", [128, 1024], F32)
    cx.P2 = ps("P2", [128, 512], F32)
    cx.P3 = ps("P3", [128, 512], F32)
    cx.PA = ps("PA", [128, 512], F32)
    cx.PB = ps("PB", [128, 512], F32)
    cx.PT = ps("PT", [128, 1024], BF16)[:, 0:128]
    cx.PT2 = ps("PT2", [128, 1024], BF16)[:, 0:128]
    cx.sems = [cx.ctx.enter_context(nc.semaphore(f"S{i}")) for i in range(4)]
    cx.em = Em(cx.sems[0])
    return cx


def f0(cx): return cx.fst[:, 0:1024]
def f1(cx): return cx.fst[:, 1024:2048]
def f2(cx): return cx.fst[:, 2048:3072]
def f3(cx): return cx.fst[:, 3072:4096]


def PBUFS(cx):
    return [cx.P0[:, 0:512], cx.P0[:, 512:1024], cx.P2[:, :], cx.P3[:, :]]


def load_const_setup(cx, d):
    em = cx.em
    em.op("vector", lambda e: e.memset(cx.onesf[:, :], 1.0))
    em.op("vector", lambda e: e.memset(cx.onesb[:, :], 1.0))
    em.op("vector", lambda e: e.memset(cx.neg6[:, :], -1e30))
    em.op("vector", lambda e: e.memset(cx.zero6[:, :], 0.0))
    em.op("vector", lambda e: e.memset(cx.scol[:, 8:9], 1e-5))
    em.dma("sync", lambda e: e.dma_start(out=cx.jrev[:, :], in_=d["jrev"]))
    em.dma("sync", lambda e: e.dma_start(out=cx.ident[:, :], in_=d["ident"]))
    em.dma("sync", lambda e: e.dma_start(out=cx.pairc[:, :], in_=d["a_pairc"]))
    em.barrier()


def load_weight_bf16(cx, dram_ap, dst, dst_off, rows, cols):
    em = cx.em
    slots = [f0(cx), f3(cx)]
    cast_tok = {}
    for i in range(rows // 128):
        sl = slots[i % 2][:, 0:cols]
        if i >= 2:
            em.wait("sync", cast_tok[i - 2])
        dt = em.dma("sync", lambda e, i=i, sl=sl: e.dma_start(
            out=sl, in_=dram_ap[i * 128:(i + 1) * 128, 0:cols]))
        em.wait("vector", dt)
        o = dst_off + i * cols
        em.op("vector", lambda e, o=o, cols=cols, sl=sl: e.tensor_copy(
            out=dst[:, o:o + cols], in_=sl))
        cast_tok[i] = em.tok("vector")
    em.barrier()


def load_bias_row(cx, dram_ap, dst_off, n):
    em = cx.em
    em.dma("sync", lambda e: e.dma_start(out=f0(cx)[0:1, 0:n], in_=dram_ap))
    em.barrier(cons=("vector",))
    em.op("vector", lambda e: e.tensor_copy(out=cx.rows[0:1, dst_off:dst_off + n],
                                            in_=f0(cx)[0:1, 0:n]))
    em.barrier()


def ln_stats_apply(cx, src, out_ap, extra_add=None):
    em = cx.em
    s1 = cx.scol[:, 0:1]; s2 = cx.scol[:, 1:2]; mcol = cx.scol[:, 2:3]
    vcol = cx.scol[:, 3:4]; v2 = cx.scol[:, 4:5]; sd = cx.scol[:, 5:6]
    rcol = cx.scol[:, 6:7]; msq = cx.scol[:, 7:8]
    em.op("vector", lambda e: e.reduce_sum(s1, src, axis=AX.X))
    em.op("scalar", lambda e: e.activation(out=f2(cx), in_=src, func=AF.Square,
                                           accum_out=s2))
    em.barrier(cons=("vector",))
    em.op("vector", lambda e: e.tensor_scalar_mul(mcol, s1, 1.0 / D))
    em.op("vector", lambda e: e.tensor_scalar_mul(vcol, s2, 1.0 / D))
    em.barrier(cons=("vector",))
    em.op("vector", lambda e: e.tensor_mul(msq, mcol, mcol))
    em.barrier(cons=("vector",))
    em.op("vector", lambda e: e.tensor_sub(v2, vcol, msq))
    em.barrier(cons=("scalar",))
    em.op("scalar", lambda e: e.activation(out=sd, in_=v2, func=AF.Sqrt, bias=cx.scol[:, 8:9]))
    em.barrier(cons=("vector",))
    em.op("vector", lambda e: e.reciprocal(rcol, sd))
    em.barrier(cons=("vector",))
    if extra_add is None:
        em.op("vector", lambda e: e.tensor_scalar(
            out=out_ap, in0=src, scalar1=mcol, scalar2=rcol,
            op0=OP.subtract, op1=OP.mult))
        em.barrier()
    else:
        em.op("vector", lambda e: e.tensor_scalar(
            out=f2(cx), in0=src, scalar1=mcol, scalar2=rcol,
            op0=OP.subtract, op1=OP.mult))
        em.op("vector", lambda e: e.tensor_add(out_ap, f2(cx), extra_add))
        em.barrier()


def ln_pass(cx, x_dram, b, dst, dst_off_fn):
    em = cx.em
    slots = [f0(cx), f3(cx)]
    em.dma("sync", lambda e: e.dma_start(out=slots[0], in_=x_dram[b * T:b * T + 128, :]))
    em.barrier()
    for tt in range(TT):
        x = slots[tt % 2]
        if tt + 1 < TT:
            em.dma("sync", lambda e, tt=tt: e.dma_start(
                out=slots[(tt + 1) % 2],
                in_=x_dram[b * T + (tt + 1) * 128:b * T + (tt + 2) * 128, :]))
        o = dst_off_fn(tt)
        ln_stats_apply(cx, x, dst[:, o:o + 1024])


def transpose_pass(cx, src, src_off_fn, dst, dst_off_fn):
    em = cx.em
    for tt in range(TT):
        for ch in range(NCH):
            so = src_off_fn(tt, ch)
            do = dst_off_fn(tt, ch)
            em.dma("sync", lambda e, so=so, do=do: e.dma_start_transpose(
                dst[:, do:do + 128], src[:, so:so + 128]))
    em.barrier()


def mm_evac_pipe(cx, jobs):
    em = cx.em
    bank_free = [None] * 4
    for i, (mms, evac, eng) in enumerate(jobs):
        bi = i % 4
        pb = PBUFS(cx)[bi]
        if bank_free[bi] is not None:
            em.wait("tensor", bank_free[bi])
        for fn in mms:
            em.op("tensor", lambda e, fn=fn, pb=pb: fn(e, pb))
        tm = em.tok("tensor")
        em.wait(eng, tm)
        em.op(eng, lambda e, evac=evac, pb=pb: evac(e, pb))
        bank_free[bi] = em.tok(eng)
    em.barrier()


def postln_tile(cx, b, tt, src_f32, xin_dram, xout_dram):
    em = cx.em
    r0 = b * T + tt * 128
    em.dma("sync", lambda e: e.dma_start(out=f0(cx), in_=xin_dram[r0:r0 + 128, :]))
    ln_stats_apply(cx, src_f32, f1(cx), extra_add=f0(cx))
    em.dma("sync", lambda e: e.dma_start(out=xout_dram[r0:r0 + 128, :], in_=f1(cx)))
    em.barrier()



def emit_l0_attn(cx, d, xin, xout):
    em = cx.em
    load_const_setup(cx, d)
    load_bias_row(cx, d["l0_buk"], 0, 1024)
    load_bias_row(cx, d["l0_buv"], 1024, 1024)
    em.dma("sync", lambda e: e.dma_start(out=cx.sbias[:, 0:1], in_=d["l0_bdk"]))
    em.dma("sync", lambda e: e.dma_start(out=cx.sbias[:, 1:2], in_=d["l0_bdv"]))
    em.barrier()
    ckv = cx.wbf[:, 4096:8192]

    for b in range(NB):
        load_weight_bf16(cx, d["l0_wdk"], cx.wbf, 0, 1024, 128)
        load_weight_bf16(cx, d["l0_wdv"], cx.wbf, 1024, 1024, 128)
        load_weight_bf16(cx, d["l0_wuk"], cx.wbf, 2048, 128, 1024)
        load_weight_bf16(cx, d["l0_wuv"], cx.wbf, 3072, 128, 1024)
        ln_pass(cx, xin, b, cx.bigA, lambda tt: tt * 1024)
        if b == 0:
            dbg(cx, "xln_tm", cx.bigA, 16384)
        transpose_pass(cx, cx.bigA, lambda tt, ch: tt * 1024 + ch * 128,
                       cx.bigC, lambda tt, ch: ch * 2048 + tt * 128)
        if b == 0:
            dbg(cx, "xln_fm", cx.bigC, 16384)
        jobs = []
        for cc in range(2):
            for tb in range(NQB):
                mms = []
                for ic in range(NCH):
                    def mm(e, pb, cc=cc, ic=ic, tb=tb):
                        return e.matmul(
                            out=pb,
                            lhsT=cx.wbf[:, cc * 1024 + ic * 128:cc * 1024 + ic * 128 + 128],
                            rhs=cx.bigC[:, ic * 2048 + tb * 512:ic * 2048 + tb * 512 + 512],
                            start=(ic == 0), stop=(ic == NCH - 1), skip_group_check=True)
                    mms.append(mm)
                def ev(e, pb, cc=cc, tb=tb):
                    return e.tensor_scalar_add(
                        ckv[:, cc * 2048 + tb * 512:cc * 2048 + tb * 512 + 512],
                        pb, cx.sbias[:, cc:cc + 1])
                jobs.append((mms, ev, "vector"))
        mm_evac_pipe(cx, jobs)
        jobs = []
        for tt in range(TT):
            for ob in range(2):
                def mm1(e, pb, tt=tt, ob=ob):
                    return e.matmul(
                        out=pb, lhsT=ckv[:, tt * 128:tt * 128 + 128],
                        rhs=cx.wbf[:, 2048 + ob * 512:2048 + ob * 512 + 512],
                        start=True, stop=False, skip_group_check=True)
                def mm2(e, pb, ob=ob):
                    return e.matmul(
                        out=pb, lhsT=cx.onesb[0:1, :],
                        rhs=cx.rows[0:1, ob * 512:ob * 512 + 512],
                        start=False, stop=True, skip_group_check=True)
                def ev(e, pb, tt=tt, ob=ob):
                    return e.activation(
                        out=cx.vh[:, tt * 1024 + ob * 512:tt * 1024 + ob * 512 + 512],
                        in_=pb, func=AF.Copy)
                jobs.append(([mm1, mm2], ev, "scalar"))
        mm_evac_pipe(cx, jobs)
        if b == 0:
            dbg(cx, "ckv", ckv, 4096)
            dbg(cx, "k_tm", cx.vh, 16384)
        for tt in range(TT):
            em.dma("sync", lambda e, tt=tt: e.dma_start(
                out=cx.aux[:, 0:1024], in_=d["rope_cos"][tt * 128:tt * 128 + 128, :]))
            em.dma("sync", lambda e, tt=tt: e.dma_start(
                out=cx.aux[:, 1024:2048], in_=d["rope_sin"][tt * 128:tt * 128 + 128, :]))
            em.barrier(cons=("vector",))
            for (buf, off) in ((cx.bigA, tt * 1024), (cx.vh, tt * 1024)):
                qv = buf[:, off:off + 1024].rearrange("p (h t s) -> p h t s", t=2, s=32)
                sv = cx.aux[:, 1024:2048].rearrange("p (h t s) -> p h t s", t=2, s=32)
                t1 = cx.rtmp[:, 0:1024]
                t2 = cx.rtmp[:, 1024:2048]
                t2v = t2.rearrange("p (h t s) -> p h t s", t=2, s=32)
                em.op("vector", lambda e, buf=buf, off=off, t1=t1: e.tensor_mul(
                    t1, buf[:, off:off + 1024], cx.aux[:, 0:1024]))
                em.op("vector", lambda e, qv=qv, sv=sv, t2v=t2v: e.tensor_mul(
                    t2v[:, :, 0, :], qv[:, :, 1, :], sv[:, :, 0, :]))
                em.op("vector", lambda e, qv=qv, sv=sv, t2v=t2v: e.tensor_mul(
                    t2v[:, :, 1, :], qv[:, :, 0, :], sv[:, :, 1, :]))
                em.op("vector", lambda e, buf=buf, off=off, t1=t1, t2=t2: e.tensor_add(
                    buf[:, off:off + 1024], t1, t2))
            em.barrier()
        if b == 0:
            dbg(cx, "qr_tm", cx.bigA, 16384)
            dbg(cx, "kr_tm", cx.vh, 16384)
        transpose_pass(cx, cx.bigA, lambda tt, ch: tt * 1024 + ch * 128,
                       cx.bigD, lambda tt, ch: ch * 2048 + tt * 128)
        transpose_pass(cx, cx.vh, lambda tt, ch: tt * 1024 + ch * 128,
                       cx.bigC, lambda tt, ch: ch * 2048 + tt * 128)
        em.op("vector", lambda e: e.memset(cx.vh[:, :], 1.0))
        em.barrier()
        for tt in range(TT):
            for ob in range(2):
                pb = PBUFS(cx)[ob]
                em.op("tensor", lambda e, tt=tt, ob=ob, pb=pb: e.matmul(
                    out=pb, lhsT=ckv[:, 2048 + tt * 128:2048 + tt * 128 + 128],
                    rhs=cx.wbf[:, 3072 + ob * 512:3072 + ob * 512 + 512],
                    start=True, stop=False, skip_group_check=True))
                em.op("tensor", lambda e, ob=ob, pb=pb: e.matmul(
                    out=pb, lhsT=cx.onesb[0:1, :],
                    rhs=cx.rows[0:1, 1024 + ob * 512:1024 + ob * 512 + 512],
                    start=False, stop=True, skip_group_check=True))
            em.barrier(cons=("scalar",))
            for ob in range(2):
                pb = PBUFS(cx)[ob]
                dst = cx.vh[:, tt * 1040 + ob * 520:tt * 1040 + (ob + 1) * 520]
                dv = dst.rearrange("p (h x) -> p h x", x=65)[:, :, 0:64]
                pv = pb.rearrange("p (h x) -> p h x", x=64)
                em.op("scalar", lambda e, dv=dv, pv=pv: e.activation(
                    out=dv, in_=pv, func=AF.Copy))
            em.barrier()
        if b == 0:
            dbg(cx, "vaug", cx.vh, 16640)
            dbg(cx, "qr_fm", cx.bigD, 16384)
            dbg(cx, "kr_fm", cx.bigC, 16384)
        attention(cx, kbuf=cx.bigC, with_bias=False)
        if b == 0:
            dbg(cx, "attn", cx.bigD, 16384)
        load_weight_bf16(cx, d["l0_wo"], cx.wbf, 0, 1024, 1024)
        load_bias_row(cx, d["l0_bo"], 2048, 1024)
        proj_postln(cx, b, xin, xout, brow_off=2048)
    em.barrier(cons=("sync",))


def attention(cx, kbuf, with_bias):
    em = cx.em
    for h in range(H):
        r0 = 64 * (h % 2)
        chq = (h // 2) * 2048
        if with_bias:
            em.dma("sync", lambda e, h=h: e.dma_start(out=cx.growb[0:1, :],
                                                      in_=cx.gall[h:h + 1, :]))
            if "tfl" not in SKIPS:
                for s, (side, delta) in enumerate(BAND_SLOTS):
                    si = 0 if side == "neg" else 1
                    base = (si * 16 + h) * 4095 + (1920 - delta)
                    win = RawAP(cx.tsrev_t, base, [[1, 128], [1, 512]])
                    em.dma("sync", lambda e, s=s, win=win: e.dma_start(
                        out=cx.aux[:, s * 512:s * 512 + 512], in_=win))
            em.barrier()
        tok_out_prev = None
        for qi in range(NQB):
            qoff = chq + qi * 512
            if with_bias:
                side = BAND_SIDES[qi]
                if any(t[0] == "rank1" for t in BAND_TILES[qi]):
                    dd = (cx.sbias[0:1, 24 + h:24 + h + 1] if side == "pos"
                          else cx.sbias[0:1, 40 + h:40 + h + 1])
                    em.op("vector", lambda e, qi=qi, dd=dd: e.tensor_scalar(
                        out=cx.rowt[0:1, :],
                        in0=cx.growb[0:1, qi * 512:qi * 512 + 512],
                        scalar1=dd, scalar2=None, op0=OP.mult))
                    em.barrier(cons=("tensor",))
                if any(t[0] == "band" for t in BAND_TILES[qi]):
                    em.op("tensor", lambda e, qi=qi: e.matmul(
                        out=cx.PB[:, :], lhsT=cx.onesb[0:1, :],
                        rhs=cx.growb[0:1, qi * 512:qi * 512 + 512],
                        start=True, stop=True, skip_group_check=True))
                    em.barrier(cons=("vector",))
                    em.op("vector", lambda e: e.tensor_copy(out=cx.gt[:, :],
                                                            in_=cx.PB[:, :]))
                    em.barrier(cons=("vector", "tensor"))
            toks_exp = {}
            toks_out = {}
            tEs = {}
            for kt in range(KT):
                half = kt % 2
                pS = cx.P0[:, half * 512:half * 512 + 512]
                t = BAND_TILES[qi][kt] if with_bias else ("skip",)
                if t[0] == "band" and "band" in SKIPS:
                    t = ("skip",)
                if t[0] == "rank1" and "rank1" in SKIPS:
                    t = ("skip",)
                if kt >= 2:
                    em.wait("tensor", toks_exp[kt - 2])
                if kt == 0 and tok_out_prev is not None:
                    em.wait("tensor", tok_out_prev)
                nojmm = "jmm" in SKIPS
                if t[0] == "band":
                    bb = cx.biasb[:, half * 512:half * 512 + 512]
                    em.barrier(cons=("vector",))
                    em.op("vector", lambda e, t=t, bb=bb: e.tensor_mul(
                        bb, cx.aux[:, t[1] * 512:t[1] * 512 + 512], cx.gt[:, :]))
                    tv = em.tok("vector")
                    em.wait("tensor", tv)
                em.op("tensor", lambda e, kt=kt, pS=pS, qoff=qoff, r0=r0, chq=chq,
                      last=(t[0] == "skip" or (t[0] == "band" and nojmm)),
                      kb=kbuf: e.matmul(
                    out=pS, lhsT=kb[r0:r0 + 64, chq + kt * 128:chq + kt * 128 + 128],
                    rhs=cx.bigD[r0:r0 + 64, qoff:qoff + 512],
                    start=True, stop=last, skip_group_check=True))
                if t[0] == "rank1":
                    em.op("tensor", lambda e, pS=pS: e.matmul(
                        out=pS, lhsT=cx.onesb[0:1, :], rhs=cx.rowt[0:1, :],
                        start=False, stop=True, skip_group_check=True))
                elif t[0] == "band" and not nojmm:
                    bb = cx.biasb[:, half * 512:half * 512 + 512]
                    em.op("tensor", lambda e, pS=pS, bb=bb: e.matmul(
                        out=pS, lhsT=cx.jrev[:, :], rhs=bb,
                        start=False, stop=True, skip_group_check=True))
                tEs[kt] = em.tok("tensor")
                em.wait("scalar", tEs[kt])
                if kt >= 2:
                    em.wait("scalar", toks_out[kt - 2])
                pt = cx.ptile[:, half * 512:half * 512 + 512]
                em.op("scalar", lambda e, pS=pS, pt=pt: e.activation(
                    out=pt, in_=pS, func=AF.Exp, scale=0.125))
                toks_exp[kt] = em.tok("scalar")
                em.wait("tensor", toks_exp[kt])
                em.op("tensor", lambda e, kt=kt, h=h, pt=pt: e.matmul(
                    out=cx.P2[0:65, :],
                    lhsT=cx.vh[:, kt * 1040 + h * 65:kt * 1040 + h * 65 + 65],
                    rhs=pt, start=(kt == 0), stop=(kt == KT - 1),
                    skip_group_check=True))
                toks_out[kt] = em.tok("tensor")
            em.wait("vector", toks_out[KT - 1])
            em.op("vector", lambda e: e.reciprocal(cx.fst[0:1, 1024:1536],
                                                   cx.P2[64:65, :]))
            em.barrier(cons=("scalar", "tensor"))
            em.op("scalar", lambda e: e.activation(out=cx.fst[0:64, 2048:2560],
                                                   in_=cx.P2[0:64, :], func=AF.Copy))
            em.op("tensor", lambda e: e.matmul(out=cx.P3[0:64, :],
                                               lhsT=cx.onesf[0:1, 0:64],
                                               rhs=cx.fst[0:1, 1024:1536],
                                               start=True, stop=True,
                                               skip_group_check=True))
            em.barrier(cons=("vector",))
            em.op("vector", lambda e, r0=r0, qoff=qoff: e.tensor_mul(
                cx.bigD[r0:r0 + 64, qoff:qoff + 512], cx.fst[0:64, 2048:2560],
                cx.P3[0:64, :]))
            tok_out_prev = em.tok("vector")
        em.barrier()


def proj_postln(cx, b, xin, xout, brow_off, src=None, dst=None):
    em = cx.em
    if src is None:
        src = cx.bigD
    for tt in range(TT):
        for ob in range(2):
            pb = PBUFS(cx)[ob]
            for ch in range(NCH):
                em.op("tensor", lambda e, tt=tt, ob=ob, ch=ch, pb=pb: e.matmul(
                    out=pb,
                    lhsT=src[:, ch * 2048 + tt * 128:ch * 2048 + tt * 128 + 128],
                    rhs=cx.wbf[:, ch * 1024 + ob * 512:ch * 1024 + ob * 512 + 512],
                    start=(ch == 0), stop=False, skip_group_check=True))
            em.op("tensor", lambda e, ob=ob, pb=pb: e.matmul(
                out=pb, lhsT=cx.onesb[0:1, :],
                rhs=cx.rows[0:1, brow_off + ob * 512:brow_off + ob * 512 + 512],
                start=False, stop=True, skip_group_check=True))
        em.barrier(cons=("scalar",))
        for ob in range(2):
            pb = PBUFS(cx)[ob]
            em.op("scalar", lambda e, ob=ob, pb=pb: e.activation(
                out=f1(cx)[:, ob * 512:ob * 512 + 512], in_=pb, func=AF.Copy))
        em.barrier()
        postln_tile(cx, b, tt, f1(cx), xin, xout)



def emit_l0_ffn(cx, d, xin, xout):
    em = cx.em
    load_const_setup(cx, d)
    load_weight_bf16(cx, d["f_bwT"], cx.wbf, 0, 1024, 1024)
    load_weight_bf16(cx, d["f_saT"], cx.ws, 0, 1024, 16)
    em.dma("sync", lambda e: e.dma_start(out=f0(cx)[0:16, :], in_=d["f_sbT"]))
    em.barrier(cons=("vector",))
    em.op("vector", lambda e: e.tensor_copy(out=cx.ws[0:16, 128:1152],
                                            in_=f0(cx)[0:16, :]))
    em.barrier()
    load_weight_bf16(cx, d["f_raT"], cx.ws, 1152, 1024, 96)
    em.dma("sync", lambda e: e.dma_start(out=f0(cx)[0:96, :], in_=d["f_rb"]))
    em.barrier(cons=("vector",))
    em.op("vector", lambda e: e.tensor_copy(out=cx.ws[0:96, 1920:2944],
                                            in_=f0(cx)[0:96, :]))
    em.barrier()
    load_weight_bf16(cx, d["f_rwT"], cx.ws, 2944, 1024, 6)
    em.dma("sync", lambda e: e.dma_start(out=cx.ws[0:6, 2992:3088], in_=d["f_esel"]))
    load_bias_row(cx, d["f_bb"], 0, 1024)
    load_bias_row(cx, d["f_rbias"], 2048, 6)
    em.barrier()

    for b in range(NB):
        ln_pass(cx, xin, b, cx.bigA, lambda tt: tt * 1024)
        transpose_pass(cx, cx.bigA, lambda tt, ch: tt * 1024 + ch * 128,
                       cx.bigC, lambda tt, ch: ch * 2048 + tt * 128)
        for tb in range(NQB):
            for ch in range(NCH):
                em.op("tensor", lambda e, tb=tb, ch=ch: e.matmul(
                    out=cx.PA[0:96, :],
                    lhsT=cx.ws[:, 1152 + ch * 96:1152 + ch * 96 + 96],
                    rhs=cx.bigC[:, ch * 2048 + tb * 512:ch * 2048 + tb * 512 + 512],
                    start=(ch == 0), stop=(ch == NCH - 1), skip_group_check=True))
            em.barrier(cons=("vector",))
            em.op("vector", lambda e, tb=tb: e.tensor_copy(
                out=cx.bigD[0:96, tb * 512:tb * 512 + 512], in_=cx.PA[0:96, :]))
            em.barrier()
        for tb in range(NQB):
            for ch in range(NCH):
                em.op("tensor", lambda e, tb=tb, ch=ch: e.matmul(
                    out=cx.PA[0:16, :], lhsT=cx.ws[:, ch * 16:ch * 16 + 16],
                    rhs=cx.bigC[:, ch * 2048 + tb * 512:ch * 2048 + tb * 512 + 512],
                    start=(ch == 0), stop=(ch == NCH - 1), skip_group_check=True))
            em.barrier(cons=("vector",))
            em.op("vector", lambda e, tb=tb: e.tensor_copy(
                out=cx.vh[0:16, tb * 512:tb * 512 + 512], in_=cx.PA[0:16, :]))
            em.barrier()
        for tt in range(TT):
            for ch in range(NCH):
                em.op("tensor", lambda e, tt=tt, ch=ch: e.matmul(
                    out=cx.PA[:, 0:6],
                    lhsT=cx.bigC[:, ch * 2048 + tt * 128:ch * 2048 + tt * 128 + 128],
                    rhs=cx.ws[:, 2944 + ch * 6:2944 + ch * 6 + 6],
                    start=(ch == 0), stop=False, skip_group_check=True))
            em.op("tensor", lambda e: e.matmul(
                out=cx.PA[:, 0:6], lhsT=cx.onesb[0:1, :], rhs=cx.rows[0:1, 2048:2054],
                start=False, stop=True, skip_group_check=True))
            em.barrier(cons=("vector",))
            g = f2(cx)[:, 0:6]
            logit = f2(cx)[:, 8:14]
            wv = f2(cx)[:, 16:22]
            tmp6 = f2(cx)[:, 24:30]
            m1 = cx.scol[:, 0:1]; m2 = cx.scol[:, 1:2]; ssum = cx.scol[:, 2:3]
            nm = cx.scol[:, 3:4]
            em.op("vector", lambda e: e.tensor_copy(out=logit, in_=cx.PA[:, 0:6]))
            em.barrier(cons=("vector",))
            em.op("vector", lambda e: e.reduce_max(m1, logit, axis=AX.X))
            em.barrier(cons=("vector",))
            em.op("vector", lambda e: e.tensor_scalar_mul(nm, m1, -1.0))
            em.barrier(cons=("scalar",))
            em.op("scalar", lambda e: e.activation(out=g, in_=logit, func=AF.Exp,
                                                   bias=nm))
            em.barrier(cons=("vector",))
            for fn in [
                lambda e: e.reduce_sum(ssum, g, axis=AX.X),
                lambda e: e.reciprocal(m2, ssum),
                lambda e: e.tensor_scalar_mul(g, g, m2),
                lambda e: e.reduce_max(m1, g, axis=AX.X),
                lambda e: e.tensor_scalar(out=cx.msk6[:, :], in0=g, scalar1=m1,
                                          scalar2=None, op0=OP.is_ge),
                lambda e: e.select(tmp6, cx.msk6[:, :], cx.neg6[:, :], g),
                lambda e: e.reduce_max(m2, tmp6, axis=AX.X),
                lambda e: e.tensor_scalar(out=cx.msk6[:, :], in0=g, scalar1=m2,
                                          scalar2=None, op0=OP.is_ge),
                lambda e: e.select(wv, cx.msk6[:, :], g, cx.zero6[:, :]),
                lambda e: e.reduce_sum(ssum, wv, axis=AX.X),
                lambda e: e.tensor_scalar_add(ssum, ssum, 1e-6),
                lambda e: e.reciprocal(m2, ssum),
                lambda e: e.memset(cx.rtmp[:, 0:128], 0.0),
                lambda e: e.tensor_scalar(out=cx.rtmp[:, 0:6], in0=wv, scalar1=m2,
                                          scalar2=None, op0=OP.mult),
            ]:
                em.op("vector", fn)
                em.barrier(cons=("vector",))
            em.barrier(cons=("tensor",))
            em.op("tensor", lambda e: e.transpose(out=cx.PT[:, :], in_=cx.rtmp[:, 0:128],
                                                  identity=cx.ident[:, :]))
            em.barrier(cons=("vector",))
            em.op("vector", lambda e: e.tensor_copy(out=cx.rtmp[:, 128:256],
                                                    in_=cx.PT[:, :]))
            em.barrier(cons=("tensor",))
            em.op("tensor", lambda e: e.matmul(out=cx.PB[0:96, 0:128],
                                               lhsT=cx.ws[0:6, 2992:3088],
                                               rhs=cx.rtmp[0:6, 128:256],
                                               start=True, stop=True,
                                               skip_group_check=True))
            em.barrier(cons=("vector",))
            em.op("vector", lambda e, tt=tt: e.tensor_mul(
                cx.bigA[0:96, tt * 128:tt * 128 + 128],
                cx.bigD[0:96, tt * 128:tt * 128 + 128],
                cx.PB[0:96, 0:128]))
            em.barrier(cons=("tensor",))
        em.barrier()
        jobs = []
        for tt in range(TT):
            for ob in range(2):
                mms = []
                for ch in range(NCH):
                    def mm(e, pb, tt=tt, ob=ob, ch=ch):
                        return e.matmul(
                            out=pb,
                            lhsT=cx.bigC[:, ch * 2048 + tt * 128:ch * 2048 + tt * 128 + 128],
                            rhs=cx.wbf[:, ch * 1024 + ob * 512:ch * 1024 + ob * 512 + 512],
                            start=(ch == 0), stop=False, skip_group_check=True)
                    mms.append(mm)
                def mms1(e, pb, tt=tt, ob=ob):
                    return e.matmul(
                        out=pb, lhsT=cx.vh[0:16, tt * 128:tt * 128 + 128],
                        rhs=cx.ws[0:16, 128 + ob * 512:128 + ob * 512 + 512],
                        start=False, stop=False, skip_group_check=True)
                def mms2(e, pb, tt=tt, ob=ob):
                    return e.matmul(
                        out=pb, lhsT=cx.bigA[0:96, tt * 128:tt * 128 + 128],
                        rhs=cx.ws[0:96, 1920 + ob * 512:1920 + ob * 512 + 512],
                        start=False, stop=False, skip_group_check=True)
                def mms3(e, pb, ob=ob):
                    return e.matmul(
                        out=pb, lhsT=cx.onesb[0:1, :],
                        rhs=cx.rows[0:1, ob * 512:ob * 512 + 512],
                        start=False, stop=True, skip_group_check=True)
                mms += [mms1, mms2, mms3]
                def ev(e, pb, tt=tt, ob=ob):
                    return e.activation(
                        out=cx.bigD[:, tt * 1024 + ob * 512:tt * 1024 + ob * 512 + 512],
                        in_=pb, func=AF.Copy)
                jobs.append((mms, ev, "scalar"))
        mm_evac_pipe(cx, jobs)
        for tt in range(TT):
            postln_tile(cx, b, tt, cx.bigD[:, tt * 1024:tt * 1024 + 1024], xin, xout)
    em.barrier(cons=("sync",))



def emit_l1_attn(cx, d, xin, xout):
    em = cx.em
    load_const_setup(cx, d)
    em.dma("sync", lambda e: e.dma_start(out=cx.sbias[0:16, 73:74], in_=d["a_cneg_col"]))
    em.dma("sync", lambda e: e.dma_start(out=cx.sbias[0:16, 74:75], in_=d["a_cpos_col"]))
    em.dma("sync", lambda e: e.dma_start(out=cx.sbias[0:8, 75:76], in_=d["a_grub"]))
    em.dma("sync", lambda e: e.dma_start(out=cx.sbias[8:16, 75:76], in_=d["a_grub"]))
    em.dma("sync", lambda e: e.dma_start(out=cx.sbias[0:1, 56:72], in_=d["a_gc"]))
    em.barrier()
    if "ts" not in SKIPS:
        build_ts_tables(cx, d)
    em.dma("sync", lambda e: e.dma_start(out=f0(cx)[:, 0:16], in_=d["a_gruwbd"]))
    em.barrier(cons=("vector",))
    em.op("vector", lambda e: e.tensor_copy(out=cx.ws[:, 0:16], in_=f0(cx)[:, 0:16]))
    em.barrier()

    for b in range(NB):
        ln_pass(cx, xin, b, cx.bigA, lambda tt: tt * 1024)
        transpose_pass(cx, cx.bigA, lambda tt, ch: tt * 1024 + ch * 128,
                       cx.bigC, lambda tt, ch: ch * 2048 + tt * 128)
        if "gates" not in SKIPS:
            gates(cx, d)
        else:
            em.op("vector", lambda e: e.memset(cx.gall[:, :], 1.0))
            em.barrier()
        load_weight_bf16(cx, d["a_wq"], cx.wbf, 0, 1024, 1024)
        qkv_fm(cx, d, "a_bq", cx.bigD)
        load_weight_bf16(cx, d["a_wk"], cx.wbf, 0, 1024, 1024)
        qkv_fm(cx, d, "a_bk", cx.bigA)
        load_weight_bf16(cx, d["a_wv"], cx.wbf, 0, 1024, 1024)
        load_bias_row(cx, d["a_bv"], 1024, 1024)
        em.op("vector", lambda e: e.memset(cx.vh[:, :], 1.0))
        em.barrier()
        jobs = []
        for tt in range(TT):
            for ob in range(2):
                mms = []
                for ic in range(NCH):
                    def mm(e, pb, tt=tt, ob=ob, ic=ic):
                        return e.matmul(
                            out=pb,
                            lhsT=cx.bigC[:, ic * 2048 + tt * 128:ic * 2048 + tt * 128 + 128],
                            rhs=cx.wbf[:, ic * 1024 + ob * 512:ic * 1024 + ob * 512 + 512],
                            start=(ic == 0), stop=False, skip_group_check=True)
                    mms.append(mm)
                def mmb(e, pb, ob=ob):
                    return e.matmul(
                        out=pb, lhsT=cx.onesb[0:1, :],
                        rhs=cx.rows[0:1, 1024 + ob * 512:1024 + ob * 512 + 512],
                        start=False, stop=True, skip_group_check=True)
                mms.append(mmb)
                def ev(e, pb, tt=tt, ob=ob):
                    dst = cx.vh[:, tt * 1040 + ob * 520:tt * 1040 + (ob + 1) * 520]
                    dv = dst.rearrange("p (h x) -> p h x", x=65)[:, :, 0:64]
                    pv = pb.rearrange("p (h x) -> p h x", x=64)
                    return e.activation(out=dv, in_=pv, func=AF.Copy)
                jobs.append((mms, ev, "scalar"))
        mm_evac_pipe(cx, jobs)
        attention(cx, kbuf=cx.bigA, with_bias=("bias" not in SKIPS))
        load_weight_bf16(cx, d["a_wo"], cx.wbf, 0, 1024, 1024)
        load_bias_row(cx, d["a_bo"], 2048, 1024)
        proj_postln(cx, b, xin, xout, brow_off=2048)
    em.barrier(cons=("sync",))


def build_ts_tables(cx, d):
    em = cx.em
    tr = cx.fst[0:16, 0:4095]
    em.dma("sync", lambda e: e.dma_start(out=tr, in_=d["a_trev"]))
    em.barrier()
    for si, col in ((0, 73), (1, 74)):
        em.op("vector", lambda e, col=col: e.tensor_scalar(
            out=cx.wbf[0:16, 0:4095], in0=tr, scalar1=cx.sbias[0:16, col:col + 1],
            scalar2=None, op0=OP.subtract))
        em.barrier(cons=("sync",))
        em.dma("sync", lambda e, si=si: e.dma_start(
            out=cx.tsrev_dram[si, :, :], in_=cx.wbf[0:16, 0:4095]))
        em.barrier()
    em.dma("sync", lambda e: e.dma_start(out=cx.fst[0:1, 0:16], in_=d["a_cneg_row"]))
    em.dma("sync", lambda e: e.dma_start(out=cx.fst[0:1, 16:32], in_=d["a_cpos_row"]))
    em.barrier()
    em.op("vector", lambda e: e.tensor_sub(cx.sbias[0:1, 24:40], cx.fst[0:1, 0:16],
                                           cx.fst[0:1, 16:32]))
    em.op("vector", lambda e: e.tensor_sub(cx.sbias[0:1, 40:56], cx.fst[0:1, 16:32],
                                           cx.fst[0:1, 0:16]))
    em.barrier()


def gates(cx, d):
    em = cx.em
    for ch in range(NCH):
        for tb in range(NQB):
            em.op("tensor", lambda e, ch=ch, tb=tb: e.matmul(
                out=cx.PA[0:16, :], lhsT=cx.ws[:, 0:16],
                rhs=cx.bigC[:, ch * 2048 + tb * 512:ch * 2048 + tb * 512 + 512],
                start=True, stop=True, skip_group_check=True))
            em.barrier(cons=("vector",))
            em.op("vector", lambda e: e.tensor_scalar_add(cx.gsb[:, :], cx.PA[0:16, :],
                                                          cx.sbias[0:16, 75:76]))
            em.barrier(cons=("tensor",))
            em.op("tensor", lambda e: e.matmul(out=cx.PB[0:4, :], lhsT=cx.pairc[:, :],
                                               rhs=cx.gsb[:, :], start=True, stop=True,
                                               skip_group_check=True))
            em.barrier(cons=("scalar",))
            em.op("scalar", lambda e: e.activation(out=cx.fst[0:4, 0:512],
                                                   in_=cx.PB[0:4, :], func=AF.Sigmoid))
            em.barrier(cons=("sync",))
            for j in range(4):
                em.dma("sync", lambda e, j=j: e.dma_start(
                    out=cx.fst[0:1, 1024 + j * 512:1024 + (j + 1) * 512],
                    in_=cx.fst[j:j + 1, 0:512]))
            em.barrier()
            for hh in range(2):
                h = 2 * ch + hh
                ga = cx.fst[0:1, 1024 + (2 * hh) * 512:1024 + (2 * hh) * 512 + 512]
                gb = cx.fst[0:1, 1024 + (2 * hh + 1) * 512:1024 + (2 * hh + 1) * 512 + 512]
                t2 = cx.fst[0:1, 3072:3584]
                t3 = cx.fst[0:1, 3584:4096]
                em.op("vector", lambda e, gb=gb, h=h, t2=t2: e.tensor_scalar(
                    out=t2, in0=gb, scalar1=cx.sbias[0:1, 56 + h:56 + h + 1],
                    scalar2=-1.0, op0=OP.mult, op1=OP.add))
                em.barrier(cons=("vector",))
                em.op("vector", lambda e, ga=ga, t2=t2, t3=t3: e.tensor_mul(t3, ga, t2))
                em.barrier(cons=("vector",))
                em.op("vector", lambda e, t3=t3: e.tensor_scalar(
                    out=cx.growb[0:1, 0:512], in0=t3, scalar1=2.0, scalar2=8.0,
                    op0=OP.add, op1=OP.mult))
                em.barrier(cons=("sync",))
                em.dma("sync", lambda e, h=h, tb=tb: e.dma_start(
                    out=cx.gall[h:h + 1, tb * 512:tb * 512 + 512],
                    in_=cx.growb[0:1, 0:512]))
                em.barrier()


def qkv_fm(cx, d, bias_key, dst):
    em = cx.em
    for oc in range(NCH):
        em.dma("sync", lambda e, oc=oc: e.dma_start(
            out=cx.sbias[:, 80 + oc:81 + oc],
            in_=d[bias_key][oc * 128:oc * 128 + 128, :]))
    em.barrier(cons=("vector",))
    jobs = []
    for oc in range(NCH):
        for tb in range(NQB):
            mms = []
            for ic in range(NCH):
                def mm(e, pb, oc=oc, tb=tb, ic=ic):
                    return e.matmul(
                        out=pb,
                        lhsT=cx.wbf[:, ic * 1024 + oc * 128:ic * 1024 + oc * 128 + 128],
                        rhs=cx.bigC[:, ic * 2048 + tb * 512:ic * 2048 + tb * 512 + 512],
                        start=(ic == 0), stop=(ic == NCH - 1), skip_group_check=True)
                mms.append(mm)
            def ev(e, pb, oc=oc, tb=tb):
                return e.tensor_scalar_add(
                    dst[:, oc * 2048 + tb * 512:oc * 2048 + tb * 512 + 512],
                    pb, cx.sbias[:, 80 + oc:81 + oc])
            jobs.append((mms, ev, "vector"))
    mm_evac_pipe(cx, jobs)



def emit_l1_ffn(cx, d, xin, xout):
    em = cx.em
    load_const_setup(cx, d)
    for i in range(NCH):
        em.dma("sync", lambda e, i=i: e.dma_start(
            out=cx.sbias[:, 8 + i:9 + i], in_=d["r_mur"][i * 128:i * 128 + 128, :]))
        em.dma("sync", lambda e, i=i: e.dma_start(
            out=cx.sbias[:, 16 + i:17 + i], in_=d["r_muk"][i * 128:i * 128 + 128, :]))
    em.barrier()
    for b in range(NB):
        ln_pass(cx, xin, b, cx.bigA, lambda tt: tt * 1024)
        transpose_pass(cx, cx.bigA, lambda tt, ch: tt * 1024 + ch * 128,
                       cx.bigC, lambda tt, ch: ch * 2048 + tt * 128)
        mix(cx, cx.bigD, mu_off=8)
        load_weight_bf16(cx, d["r_wrT"], cx.wbf, 0, 1024, 1024)
        fm_matmul_act(cx, cx.bigD, cx.bigA, AF.Sigmoid)
        mix(cx, cx.bigD, mu_off=16)
        load_weight_bf16(cx, d["r_wkT"], cx.wbf, 0, 1024, 1024)
        fm_matmul_act(cx, cx.bigD, cx.vh, AF.Relu)
        for ch in range(NCH):
            em.op("scalar", lambda e, ch=ch: e.activation(
                out=cx.bigD[:, ch * 2048:ch * 2048 + 2048],
                in_=cx.vh[:, ch * 2048:ch * 2048 + 2048], func=AF.Square))
        em.barrier()
        transpose_pass(cx, cx.bigA, lambda tt, ch: ch * 2048 + tt * 128,
                       cx.bigC, lambda tt, ch: tt * 1024 + ch * 128)
        load_weight_bf16(cx, d["r_wvT"], cx.wbf, 0, 1024, 1024)
        for tt in range(TT):
            for ob in range(2):
                pb = PBUFS(cx)[ob]
                for ic in range(NCH):
                    em.op("tensor", lambda e, tt=tt, ob=ob, ic=ic, pb=pb: e.matmul(
                        out=pb,
                        lhsT=cx.bigD[:, ic * 2048 + tt * 128:ic * 2048 + tt * 128 + 128],
                        rhs=cx.wbf[:, ic * 1024 + ob * 512:ic * 1024 + ob * 512 + 512],
                        start=(ic == 0), stop=(ic == NCH - 1), skip_group_check=True))
            em.barrier(cons=("vector",))
            for ob in range(2):
                pb = PBUFS(cx)[ob]
                em.op("vector", lambda e, tt=tt, ob=ob, pb=pb: e.tensor_mul(
                    f1(cx)[:, ob * 512:ob * 512 + 512], pb,
                    cx.bigC[:, tt * 1024 + ob * 512:tt * 1024 + ob * 512 + 512]))
            em.barrier()
            postln_tile(cx, b, tt, f1(cx), xin, xout)
    em.barrier(cons=("sync",))


def mix(cx, dst, mu_off):
    em = cx.em
    for ch in range(NCH):
        o = ch * 2048
        dd = cx.rtmp[:, 0:2048]
        em.op("vector", lambda e, o=o, dd=dd: e.tensor_sub(
            dd[:, 1:2048], cx.bigC[:, o + 1:o + 2048], cx.bigC[:, o:o + 2047]))
        mu = cx.sbias[:, mu_off + ch:mu_off + ch + 1]
        em.op("vector", lambda e, o=o, dd=dd, mu=mu: e.scalar_tensor_tensor(
            out=dst[:, o + 1:o + 2048], in0=dd[:, 1:2048], scalar=mu,
            in1=cx.bigC[:, o:o + 2047], op0=OP.mult, op1=OP.add))
        em.op("vector", lambda e, o=o, mu=mu: e.tensor_scalar_mul(
            dst[:, o:o + 1], cx.bigC[:, o:o + 1], mu))
    em.barrier()


def fm_matmul_act(cx, src, dst, func):
    em = cx.em
    for oc in range(NCH):
        for tb in range(NQB):
            pb = PBUFS(cx)[tb]
            for ic in range(NCH):
                em.op("tensor", lambda e, oc=oc, tb=tb, ic=ic, pb=pb: e.matmul(
                    out=pb,
                    lhsT=cx.wbf[:, ic * 1024 + oc * 128:ic * 1024 + oc * 128 + 128],
                    rhs=src[:, ic * 2048 + tb * 512:ic * 2048 + tb * 512 + 512],
                    start=(ic == 0), stop=(ic == NCH - 1), skip_group_check=True))
        em.barrier(cons=("scalar",))
        for tb in range(NQB):
            pb = PBUFS(cx)[tb]
            em.op("scalar", lambda e, oc=oc, tb=tb, pb=pb, func=func: e.activation(
                out=dst[:, oc * 2048 + tb * 512:oc * 2048 + tb * 512 + 512],
                in_=pb, func=func))
        em.barrier()


import os
DBG_TAG = os.environ.get("KDBG", "")
SKIPS = set(os.environ.get("KSKIP", "").split(","))


def dbg(cx, tag, buf, ncols):
    if DBG_TAG != tag:
        return
    em = cx.em
    em.barrier()
    for part in range(0, ncols, 2048):
        w = min(2048, ncols - part)
        em.dma("gpsimd", lambda e, part=part, w=w: e.dma_start(
            out=cx.dbg_dram[:, part:part + w], in_=buf[:, part:part + w]))
    em.barrier()



STAGES = ("l0a", "l0f", "l1a", "l1f")
EMITTERS = {"l0a": emit_l0_attn, "l0f": emit_l0_ffn,
            "l1a": emit_l1_attn, "l1f": emit_l1_ffn}
_NC_CACHE = {}


def build_stage_nc(stage, hc_shapes):
    key = stage
    if key in _NC_CACHE:
        return _NC_CACHE[key]
    nc = bass.Bass("TRN2", target_bir_lowering=False)
    xin = nc.dram_tensor("xin", [NB * T, D], F32, kind="ExternalInput").ap()
    xout = nc.dram_tensor("xout", [NB * T, D], F32, kind="ExternalOutput").ap()
    d = {}
    for k in STAGE_KEYS[stage]:
        sh, dt = hc_shapes[k]
        d[k] = nc.dram_tensor(k, list(sh), dt, kind="ExternalInput").ap()
    cx = make_ctx(nc)
    cx.dbg_dram = nc.dram_tensor("dbgout", [128, 16640], BF16,
                                 kind="ExternalOutput").ap()
    if stage == "l1a":
        th = nc.dram_tensor("tsrev", [2, 16, 4095], BF16, kind="Internal")
        cx.tsrev_dram = th.ap()
        cx.tsrev_t = cx.tsrev_dram.tensor
    EMITTERS[stage](cx, d, xin, xout)
    with nc.Block() as blk:
        cx.em.replay(blk)
    cx.ctx.close()
    _NC_CACHE[key] = nc
    return nc


ALL_KEYS = sorted({k for ks in STAGE_KEYS.values() for k in ks})


def build_fused_nc(hc_shapes):
    if "fused" in _NC_CACHE:
        return _NC_CACHE["fused"]
    nc = bass.Bass("TRN2", target_bir_lowering=False)
    xin = nc.dram_tensor("xin", [NB * T, D], F32, kind="ExternalInput").ap()
    xout = nc.dram_tensor("xout", [NB * T, D], F32, kind="ExternalOutput").ap()
    xm = [nc.dram_tensor(f"xmid{i}", [NB * T, D], F32, kind="Internal").ap()
          for i in range(3)]
    d = {}
    for k in ALL_KEYS:
        sh, dt = hc_shapes[k]
        d[k] = nc.dram_tensor(k, list(sh), dt, kind="ExternalInput").ap()
    cx = make_ctx(nc)
    cx.dbg_dram = nc.dram_tensor("dbgout", [128, 16640], BF16,
                                 kind="ExternalOutput").ap()
    th = nc.dram_tensor("tsrev", [2, 16, 4095], BF16, kind="Internal")
    cx.tsrev_dram = th.ap()
    cx.tsrev_t = cx.tsrev_dram.tensor
    emit_l0_attn(cx, d, xin, xm[0])
    cx.em.new_epoch(cx.sems[1])
    emit_l0_ffn(cx, d, xm[0], xm[1])
    cx.em.new_epoch(cx.sems[2])
    emit_l1_attn(cx, d, xm[1], xm[2])
    cx.em.new_epoch(cx.sems[3])
    emit_l1_ffn(cx, d, xm[2], xout)
    with nc.Block() as blk:
        cx.em.replay(blk)
    cx.ctx.close()
    _NC_CACHE["fused"] = nc
    return nc


def kernel(**inputs):
    hc = host_prep(inputs)
    x = np.ascontiguousarray(np.asarray(inputs["L0_attn_x"], np.float32)
                             .reshape(16 * T, D))
    shards = [x[c * NB * T:(c + 1) * NB * T, :] for c in range(NCORE)]
    hc_shapes = {k: (v.shape, BF16 if v.dtype == np.dtype(BF) else F32)
                 for k, v in hc.items()}
    if os.environ.get("KUNFUSED"):
        cur = shards
        for stage in STAGES:
            nc = build_stage_nc(stage, hc_shapes)
            in_maps = []
            for c in range(NCORE):
                m = {"xin": cur[c]}
                for k in STAGE_KEYS[stage]:
                    m[k] = hc[k]
                in_maps.append(m)
            res = run_bass_kernel_spmd(nc, in_maps, core_ids=list(range(NCORE)))
            cur = [np.asarray(r["xout"], np.float32) for r in res.results]
        return np.concatenate(cur, axis=0).reshape(16, T, D)
    nc = build_fused_nc(hc_shapes)
    in_maps = []
    for c in range(NCORE):
        m = {"xin": shards[c]}
        for k in ALL_KEYS:
            m[k] = hc[k]
        in_maps.append(m)
    res = run_bass_kernel_spmd(nc, in_maps, core_ids=list(range(NCORE)))
    cur = [np.asarray(r["xout"], np.float32) for r in res.results]
    return np.concatenate(cur, axis=0).reshape(16, T, D)
